# version 04
# baseline (speedup 1.0000x reference)
import sys
import numpy as np

for _p in ("/opt/trn_rl_repo",):
    if _p not in sys.path:
        sys.path.insert(0, _p)
import concourse.bass as bass
import concourse.mybir as mybir
from concourse.bass_utils import run_bass_kernel_spmd
import ml_dtypes
from contextlib import ExitStack

F32 = mybir.dt.float32
BF16 = mybir.dt.bfloat16
I32 = mybir.dt.int32
U8 = mybir.dt.uint8
AF = mybir.ActivationFunctionType
ALU = mybir.AluOpType

HID = 4096
H = 128
PD = 64
NST = 128
G = 8
KTAP = 4
CH = 128
INTER = 8192
CONV_DIM = 10240
PROJ = 18560
EPS = 1e-5
L = 16384
NCORES = 8
T_OWN = 2048
NCH_OWN = 16
T_REC = 6272
NCH_REC = 49
WTOK = 1024
KC = 32
J_DT = 144
FFN_I = 11008
NI = 86


class B:

    def __init__(self, stop=None, races=True):
        self.stop = stop
        self.nc = bass.Bass("TRN2", target_bir_lowering=False, num_devices=8,
                            detect_race_conditions=races)
        self.st = ExitStack()
        self.dbg = {}

    class Ctr:
        def __init__(self, st, nc, name):
            self.h = st.enter_context(nc.semaphore(name))
            self.v = 0

    def ctr(self, name):
        return B.Ctr(self.st, self.nc, name)

    class EvPool:
        def __init__(self, st, nc, name, depth=24):
            self.sems = [st.enter_context(nc.semaphore(f"{name}{i}")) for i in range(depth)]
            self.tot = [0] * depth
            self.i = 0
            self.depth = depth

        def event(self, n_dmas):
            j = self.i % self.depth
            self.i += 1
            self.tot[j] += 16 * n_dmas
            return self.sems[j], self.tot[j]

        def totals(self):
            return [(self.sems[j], self.tot[j]) for j in range(self.depth) if self.tot[j]]

    def evpool(self, name, depth=24):
        return B.EvPool(self.st, self.nc, name, depth)

    def sb(self, name, shape, dt, stack=None):
        return (stack or self.st).enter_context(self.nc.sbuf_tensor(name, shape, dt))

    def psum(self, name, shape, dt=F32):
        return self.st.enter_context(self.nc.psum_tensor(name, shape, dt))

    def din(self, name, shape, dt=F32):
        return self.nc.dram_tensor(name, shape, dt, kind="ExternalInput").ap()

    def dscr(self, name, shape, dt):
        return self.nc.dram_tensor(name, shape, dt)

    def dsh(self, name, shape, dt):
        return self.nc.dram_tensor(name, shape, dt, addr_space="Shared")


def bview(ap, dup):
    return ap.unsqueeze(2).to_broadcast([ap.shape[0], ap.shape[1], dup])


def build(stop=None, dbg=(), dev_rec=None, dev_own=None, dev_w=None, races=True, sim_poll=True):
    b = B(stop, races=races)
    nc = b.nc
    st = b.st

    own_x = b.din("own_x", [3 + T_OWN, HID])
    rec_x = b.din("rec_x", [T_REC, HID])
    w_int = b.din("w_int", [PROJ // 128, 128, HID // 128, 128])
    w_outT = b.din("w_outT", [INTER, HID])
    conv_w4 = b.din("conv_w4", [128, 80, KTAP])
    conv_bc = b.din("conv_bc", [128, 80])
    dtb_col = b.din("dtb_col", [128, 1])
    alog_col = b.din("alog_col", [128, 1])
    d_cols = b.din("d_cols", [128, 64])
    normw_cols = b.din("normw_cols", [128, 64])
    rmask_cols = b.din("rmask_cols", [128, NCH_REC])
    use_cols = b.din("use_cols", [128, 3])
    nonce = b.din("nonce", [1, 2], I32)
    trimask = b.din("trimask", [128, 128])
    ones_row = b.din("ones_row", [1, 128])
    ident_bf = nc.dram_tensor("ident_bf", [128, 128], BF16, kind="ExternalInput").ap()
    ident_f = b.din("ident_f", [128, 128])
    gq_pk = b.din("gq_pk", [HID // 2, FFN_I // 2], U8)
    uq_pk = b.din("uq_pk", [HID // 2, FFN_I // 2], U8)
    dq_pk = b.din("dq_pk", [FFN_I // 2, HID // 2], U8)
    g_zsT = b.din("g_zsT", [128, FFN_I // 2])
    u_zsT = b.din("u_zsT", [128, FFN_I // 2])
    d_zsT = b.din("d_zsT", [344, HID // 2])
    ind4_i = b.din("ind4_i", [4, 128])

    out = nc.dram_tensor("out", [T_OWN, HID], F32, kind="ExternalOutput").ap()

    xn_scr = b.dscr("xn_scr", [T_OWN, HID], BF16)
    xnr_scr = b.dscr("xnr_scr", [WTOK, HID], BF16)
    halo_scr = b.dscr("halo_scr", [16, HID], BF16)
    sp_gate = b.dscr("sp_gate", [INTER, T_OWN], BF16)
    sp_hs = b.dscr("sp_hs", [INTER, T_OWN], BF16)
    sp_B = b.dscr("sp_B", [G * NST, T_OWN], BF16)
    sp_C = b.dscr("sp_C", [G * NST, T_OWN], BF16)
    sp_dto = b.dscr("sp_dto", [H, T_OWN], F32)
    sp_hsr = b.dscr("sp_hsr", [INTER, T_REC], BF16)
    sp_Br = b.dscr("sp_Br", [G * NST, T_REC], BF16)
    sp_dtr = b.dscr("sp_dtr", [H, T_REC], F32)
    sp_Z = b.dscr("sp_Z", [NCH_OWN, NST, INTER], BF16)
    sp_h = b.dscr("sp_h", [FFN_I, T_OWN], BF16)
    sp_down = b.dscr("sp_down", [T_OWN, HID], F32)
    sp_norm = b.dscr("sp_norm", [INTER, T_OWN], BF16)
    sp_aflat = b.dscr("sp_aflat", [H, CH], F32)
    sp_wg = b.dscr("sp_wg", [128, 43, 32, 128], BF16)
    sp_wu = b.dscr("sp_wu", [128, 43, 32, 128], BF16)
    sp_wd = b.dscr("sp_wd", [FFN_I, HID // 2], BF16)
    dum_scr = b.dscr("dum_scr", [NST, INTER], F32)
    dum_scr2 = b.dscr("dum_scr2", [H, 1], F32)
    dum_scr3 = b.dscr("dum_scr3", [1, 2], I32)

    sh_wg = b.dsh("sh_wg", [128, 86, 32, 128], BF16)
    sh_wu = b.dsh("sh_wu", [128, 86, 32, 128], BF16)
    sh_wd = b.dsh("sh_wd", [FFN_I, HID], BF16)
    sh_S = b.dsh("sh_S", [3, NST, INTER], F32)
    sh_A = b.dsh("sh_A", [3, H, 1], F32)
    sh_flags = b.dsh("sh_flags", [8, 2], I32)

    pg = b.evpool("pg", 24)
    pa = b.evpool("pa", 24)
    psy = b.evpool("py", 24)
    cpe = b.ctr("cpe")
    cdve = b.ctr("cdve")
    cact = b.ctr("cact")
    cpool = b.ctr("cpool")
    chs = b.ctr("chs")

    cw4_sb = b.sb("cw4_sb", [128, 80, KTAP], F32)
    cb_sb = b.sb("cb_sb", [128, 80], F32)
    dtb_sb = b.sb("dtb_sb", [128, 1], F32)
    nega_sb = b.sb("nega_sb", [128, 1], F32)
    dcols_sb = b.sb("dcols_sb", [128, 64], F32)
    nw_sb = b.sb("nw_sb", [128, 64], F32)
    rmask_sb = b.sb("rmask_sb", [128, NCH_REC], F32)
    use_sb = b.sb("use_sb", [128, 3], F32)
    non_sb = b.sb("non_sb", [1, 2], I32)
    tri_sb = b.sb("tri_sb", [128, 128], F32)
    ones_sb = b.sb("ones_sb", [1, 128], F32)
    id_sb = b.sb("id_sb", [128, 128], BF16)
    idf_sb = b.sb("idf_sb", [128, 128], F32)
    zeros_sb = b.sb("zeros_sb", [128, 128], F32)
    eps_sb = b.sb("eps_sb", [128, 1], F32)
    flag_sb = b.sb("flag_sb", [1, 2], I32)
    conv_tails = b.sb("conv_tails", [128, 80, 3], BF16)

    ps = [b.psum(f"ps{i}", [128, 512]) for i in range(6)]
    ps_e = b.psum("ps_e", [128, 512])
    ps_t = b.psum("ps_t", [128, 1024], BF16)

    blk = st.enter_context(nc.Block())

    ev_c = pg.event(13)
    @blk.gpsimd
    def _(g):
        for dst, src in [(cw4_sb, conv_w4), (cb_sb, conv_bc), (dtb_sb, dtb_col),
                         (nega_sb, alog_col), (dcols_sb, d_cols), (nw_sb, normw_cols),
                         (rmask_sb, rmask_cols), (use_sb, use_cols), (non_sb, nonce),
                         (tri_sb, trimask), (ones_sb, ones_row), (id_sb, ident_bf),
                         (idf_sb, ident_f)]:
            g.dma_start(out=dst[:], in_=src).then_inc(ev_c[0], 16)

    @blk.scalar
    def _(s):
        s.wait_ge(*ev_c)
        s.activation(nega_sb[:], nega_sb[:], AF.Exp).then_inc(cact.h, 1)
    cact.v += 1

    @blk.vector
    def _(v, wait=cact.v):
        v.wait_ge(cact.h, wait)
        v.tensor_scalar(nega_sb[:], nega_sb[:], -1.0, None, ALU.mult)
        v.memset(zeros_sb[:], 0.0)
        v.memset(eps_sb[:], EPS)
        v.memset(conv_tails[:], 0.0).then_inc(cdve.h, 1)
    cdve.v += 1
    consts_done = cdve.v

    bst = ExitStack()
    x_tile = b.sb("x_tile", [128, HID], F32, bst)
    sq_scr = b.sb("sq_scr", [128, HID], F32, bst)
    xn_out = b.sb("xn_out", [128, 2, HID], BF16, bst)
    ln_a = b.sb("ln_a", [128, 1], F32, bst)
    ln_b2 = b.sb("ln_b2", [128, 1], F32, bst)
    ln_m = b.sb("ln_m", [128, 1], F32, bst)
    ln_r = b.sb("ln_r", [128, 1], F32, bst)
    ln_mr = b.sb("ln_mr", [128, 1], F32, bst)
    xnT = b.sb("xnT", [128, KC, WTOK], BF16, bst)
    xnT_halo = b.sb("xnT_halo", [128, KC, 16], BF16, bst)
    wt_sb = [b.sb(f"wt{i}", [128, KC, 128], BF16, bst) for i in range(2)]
    conv_in = b.sb("conv_in", [128, 3 + WTOK], BF16, bst)
    cv = b.sb("cv", [128, WTOK], BF16, bst)
    hbc_s = b.sb("hbc_s", [128, 2, WTOK], BF16, bst)
    gev = b.sb("gev", [128, 2, WTOK], BF16, bst)
    dt_st = b.sb("dt_st", [128, WTOK], F32, bst)
    dt_sp = b.sb("dt_sp", [128, WTOK], F32, bst)
    sp_t1 = b.sb("sp_t1", [128, WTOK], F32, bst)
    halo_tm = b.sb("halo_tm", [3, HID], BF16, bst)
    halo_st = b.sb("halo_st", [3, 128], BF16, bst)

    NOEV = (None, 0)
    m = {
        "x_free": (cdve, 0),
        "x_ld": NOEV,
        "xn_free": [NOEV, NOEV],
        "wt_ld": [NOEV, NOEV],
        "wt_free": [0, 0],
        "bank_free": {0: 0, 2: 0},
        "pse_free": 0,
        "hbc_free": [NOEV, NOEV],
        "gev_free": [NOEV, NOEV],
        "conv_free": 0,
        "cv_free": 0,
        "dt_free": NOEV,
        "xnT_ok": NOEV,
        "xnT_free": 0,
        "scr_w": (),
        "tails_w": 0,
    }

    def wge(e, ev):
        if ev[0] is not None:
            e.wait_ge(ev[0], ev[1])

    def emit_ln(src, row0, nrows, scr, scr_row0, halo=False):
        ntile = (nrows + 127) // 128
        scr_evs = []
        for it in range(ntile):
            r = row0 + it * 128
            pr = min(128, nrows - it * 128)
            ob = it % 2
            ld = pa.event(1)
            @blk.scalar
            def _(s, r=r, pr=pr, xf=m["x_free"], ld=ld):
                s.wait_ge(xf[0].h, xf[1]) if xf[1] else None
                s.dma_start(out=x_tile[0:pr, :], in_=src[r:r + pr, :]).then_inc(ld[0], 16)
            m["x_ld"] = ld
            @blk.scalar
            def _(s, pr=pr, ld=ld, xnw=m["xn_free"][ob]):
                s.wait_ge(*ld)
                wge(s, xnw)
                s.activation(sq_scr[0:pr, :], x_tile[0:pr, :], AF.Square,
                             accum_out=ln_b2[0:pr, :]).then_inc(cact.h, 1)
            cact.v += 1
            @blk.vector
            def _(v, pr=pr, wait=cact.v, v0=cdve.v):
                v.wait_ge(cact.h, wait)
                k = v0
                for i, op in enumerate([
                    lambda: v.tensor_scalar(x_tile[0:pr, :], x_tile[0:pr, :], 1.0, 0.0,
                                            ALU.mult, ALU.add, accum_out=ln_a[0:pr, :]),
                    lambda: v.tensor_scalar(ln_m[0:pr, :], ln_a[0:pr, :], 1.0 / HID, None, ALU.mult),
                    lambda: v.tensor_tensor(ln_r[0:pr, :], ln_m[0:pr, :], ln_m[0:pr, :], ALU.mult),
                    lambda: v.tensor_scalar(ln_b2[0:pr, :], ln_b2[0:pr, :], 1.0 / HID, None, ALU.mult),
                    lambda: v.tensor_tensor(ln_r[0:pr, :], ln_b2[0:pr, :], ln_r[0:pr, :], ALU.subtract),
                    lambda: v.tensor_scalar(ln_r[0:pr, :], ln_r[0:pr, :], 0.0, None, ALU.max),
                ]):
                    op().then_inc(cdve.h, 1)
                    k += 1
                    v.wait_ge(cdve.h, k)
            cdve.v += 6
            @blk.scalar
            def _(s, pr=pr, wait=cdve.v, a0=cact.v):
                s.wait_ge(cdve.h, wait)
                s.activation(ln_r[0:pr, :], ln_r[0:pr, :], AF.Ln,
                             bias=eps_sb[0:pr, :]).then_inc(cact.h, 1)
                s.wait_ge(cact.h, a0 + 1)
                s.activation(ln_r[0:pr, :], ln_r[0:pr, :], AF.Exp, scale=-0.5).then_inc(cact.h, 1)
            cact.v += 2
            @blk.vector
            def _(v, pr=pr, ob=ob, wait=cact.v, halo=halo, v0=cdve.v):
                v.wait_ge(cact.h, wait)
                v.tensor_tensor(ln_mr[0:pr, :], ln_m[0:pr, :], ln_r[0:pr, :],
                                ALU.mult).then_inc(cdve.h, 1)
                v.wait_ge(cdve.h, v0 + 1)
                v.tensor_scalar(ln_mr[0:pr, :], ln_mr[0:pr, :], -1.0, None,
                                ALU.mult).then_inc(cdve.h, 1)
                v.wait_ge(cdve.h, v0 + 2)
                o = halo_tm[0:pr, :] if halo else xn_out[0:pr, ob, :]
                v.tensor_scalar(o, x_tile[0:pr, :], ln_r[0:pr, :], ln_mr[0:pr, :],
                                ALU.mult, ALU.add).then_inc(cdve.h, 1)
            cdve.v += 3
            m["x_free"] = (cdve, cdve.v)
            scr_ev = pa.event(1)
            scr_evs.append(scr_ev)
            @blk.scalar
            def _(s, it=it, pr=pr, ob=ob, wait=cdve.v, halo=halo, scr=scr,
                  scr_row0=scr_row0, scr_ev=scr_ev):
                s.wait_ge(cdve.h, wait)
                if halo:
                    s.dma_start(out=scr.ap()[13:16, :], in_=halo_tm[0:3, :]).then_inc(scr_ev[0], 16)
                else:
                    s.dma_start(out=scr.ap()[scr_row0 + it * 128:scr_row0 + it * 128 + pr, :],
                                in_=xn_out[0:pr, ob, :]).then_inc(scr_ev[0], 16)
            m["xn_free"][ob] = scr_ev
        m["scr_w"] = tuple(scr_evs)

    def emit_xnT(scr, row0, ntok, dst, ncol=KC):
        tev = psy.event(ncol)
        for kc in range(ncol):
            @blk.sync
            def _(s, kc=kc, sws=m["scr_w"], xf=m["xnT_free"], tev=tev,
                  row0=row0, ntok=ntok, scr=scr):
                if kc == 0:
                    for sw in sws:
                        s.wait_ge(*sw)
                    if xf:
                        s.wait_ge(cpe.h, xf)
                s.dma_start_transpose(dst[:, kc, 0:ntok],
                                      scr.ap()[row0:row0 + ntok, kc * 128:(kc + 1) * 128]
                                      ).then_inc(tev[0], 16)
        m["xnT_ok"] = tev

    def conv_emit(bb, wofs, ntok, is_rec, evac_wait):
        @blk.vector
        def _(v, bb=bb, ntok=ntok, wait=max(evac_wait, m["cv_free"]), v0=cdve.v):
            v.wait_ge(cact.h, wait)
            kk = v0
            v.tensor_copy(conv_in[:, 0:3], conv_tails[:, bb, :]).then_inc(cdve.h, 1)
            kk += 1
            v.wait_ge(cdve.h, kk)
            v.tensor_scalar(cv[:, 0:ntok], conv_in[:, 0:ntok], cw4_sb[:, bb, 0:1],
                            None, ALU.mult).then_inc(cdve.h, 1)
            kk += 1
            v.wait_ge(cdve.h, kk)
            for k in range(1, KTAP):
                v.scalar_tensor_tensor(cv[:, 0:ntok], conv_in[:, k:k + ntok],
                                       cw4_sb[:, bb, k:k + 1], cv[:, 0:ntok],
                                       ALU.mult, ALU.add).then_inc(cdve.h, 1)
                kk += 1
                v.wait_ge(cdve.h, kk)
            v.tensor_copy(conv_tails[:, bb, :], conv_in[:, ntok:ntok + 3]).then_inc(cdve.h, 1)
        cdve.v += 6
        m["conv_free"] = cdve.v
        hb = bb % 2
        @blk.scalar
        def _(s, bb=bb, ntok=ntok, hb=hb, wait=cdve.v, hw=m["hbc_free"][hb]):
            s.wait_ge(cdve.h, wait)
            wge(s, hw)
            s.activation(hbc_s[:, hb, 0:ntok], cv[:, 0:ntok], AF.Silu,
                         bias=cb_sb[:, bb:bb + 1]).then_inc(cact.h, 1)
        cact.v += 1
        m["cv_free"] = cact.v
        if is_rec:
            tgt, row = (sp_hsr, bb * 128) if bb < 64 else (sp_Br, (bb - 64) * 128)
        else:
            tgt, row = (sp_hs, bb * 128) if bb < 64 else \
                       ((sp_B, (bb - 64) * 128) if bb < 72 else (sp_C, (bb - 72) * 128))
        sev = pa.event(1)
        @blk.scalar
        def _(s, tgt=tgt, row=row, wofs=wofs, ntok=ntok, hb=hb, wait=cact.v, sev=sev):
            s.wait_ge(cact.h, wait)
            s.dma_start(out=tgt.ap()[row:row + 128, wofs:wofs + ntok],
                        in_=hbc_s[:, hb, 0:ntok]).then_inc(sev[0], 16)
        m["hbc_free"][hb] = sev

    def emit_window(is_rec, wofs, ntok, with_halo, halo_ev=None):
        if is_rec:
            jlist = [J_DT] + list(range(64, 136))
        else:
            jlist = [J_DT] + list(range(64, 144)) + list(range(0, 64))
        nt5 = (ntok + 511) // 512
        for ji, jb in enumerate(jlist):
            buf = ji % 2
            wev = pg.event(1)
            @blk.gpsimd
            def _(g, jb=jb, buf=buf, wfree=m["wt_free"][buf], wev=wev):
                if wfree:
                    g.wait_ge(cpe.h, wfree)
                g.dma_start(out=wt_sb[buf][:],
                            in_=w_int[jb]).then_inc(wev[0], 16)
            m["wt_ld"][buf] = wev
            bank0 = 0 if ji % 2 == 0 else 2
            is_hbc = 64 <= jb < 144
            do_halo = with_halo and is_hbc
            @blk.tensor
            def _(t, jb=jb, buf=buf, bank0=bank0, nt5=nt5, ntok=ntok,
                  wev=wev, xev=m["xnT_ok"], bwait=m["bank_free"][bank0],
                  pwait=m["pse_free"] if do_halo else 0, do_halo=do_halo,
                  halo_ev=halo_ev if do_halo else None):
                t.wait_ge(*wev)
                t.wait_ge(*xev)
                if bwait:
                    t.wait_ge(cact.h, bwait)
                for t5 in range(nt5):
                    n = min(512, ntok - t5 * 512)
                    for kc in range(KC):
                        ins = t.matmul(ps[bank0 + t5][:, 0:n], wt_sb[buf][:, kc, :],
                                       xnT[:, kc, t5 * 512:t5 * 512 + n],
                                       start=(kc == 0), stop=(kc == KC - 1))
                        if t5 == nt5 - 1 and kc == KC - 1:
                            ins.then_inc(cpe.h, 1)
                if do_halo:
                    if pwait:
                        t.wait_ge(cact.h, pwait)
                    if halo_ev is not None:
                        t.wait_ge(*halo_ev)
                    for kc in range(KC):
                        ins = t.matmul(ps_e[0:3, 0:128], xnT_halo[:, kc, 13:16],
                                       wt_sb[buf][:, kc, :],
                                       start=(kc == 0), stop=(kc == KC - 1))
                        if kc == KC - 1:
                            ins.then_inc(cpe.h, 1)
            cpe.v += 1 + (1 if do_halo else 0)
            m["wt_free"][buf] = cpe.v
            pe_main = cpe.v - (1 if do_halo else 0)
            pe_halo = cpe.v if do_halo else 0

            if do_halo:
                @blk.scalar
                def _(s, wait=pe_halo):
                    s.wait_ge(cpe.h, wait)
                    s.activation(halo_st[0:3, :], ps_e[0:3, 0:128], AF.Copy).then_inc(cact.h, 1)
                cact.v += 1
                m["pse_free"] = cact.v
                @blk.tensor
                def _(t, wait=cact.v, tw=m["tails_w"]):
                    t.wait_ge(cact.h, wait)
                    if tw:
                        t.wait_ge(cdve.h, tw)
                    t.transpose(ps_t[:, 0:3], halo_st[0:3, :], id_sb[0:3, 0:3]).then_inc(cpe.h, 1)
                cpe.v += 1
                @blk.vector
                def _(v, jb=jb, wait=cpe.v):
                    v.wait_ge(cpe.h, wait)
                    v.tensor_copy(conv_tails[:, jb - 64, :], ps_t[:, 0:3]).then_inc(cdve.h, 1)
                cdve.v += 1
                m["tails_w"] = cdve.v

            if jb == J_DT:
                @blk.scalar
                def _(s, nt5=nt5, ntok=ntok, bank0=bank0, wait=pe_main, dtw=m["dt_free"]):
                    s.wait_ge(cpe.h, wait)
                    wge(s, dtw)
                    for t5 in range(nt5):
                        n = min(512, ntok - t5 * 512)
                        s.activation(dt_st[:, t5 * 512:t5 * 512 + n], ps[bank0 + t5][:, 0:n],
                                     AF.Copy).then_inc(cact.h, 1)
                cact.v += nt5
                m["bank_free"][bank0] = cact.v
                @blk.vector
                def _(v, ntok=ntok, wait=cact.v):
                    v.wait_ge(cact.h, wait)
                    v.tensor_scalar(dt_st[:, 0:ntok], dt_st[:, 0:ntok], dtb_sb[:, 0:1],
                                    None, ALU.add).then_inc(cdve.h, 1)
                cdve.v += 1
                @blk.scalar
                def _(s, ntok=ntok, wait=cdve.v, a0=cact.v):
                    s.wait_ge(cdve.h, wait)
                    s.activation(sp_t1[:, 0:ntok], dt_st[:, 0:ntok], AF.Abs).then_inc(cact.h, 1)
                    s.wait_ge(cact.h, a0 + 1)
                    s.activation(sp_t1[:, 0:ntok], sp_t1[:, 0:ntok], AF.Exp,
                                 scale=-1.0).then_inc(cact.h, 1)
                    s.wait_ge(cact.h, a0 + 2)
                    s.activation(sp_t1[:, 0:ntok], sp_t1[:, 0:ntok], AF.Ln,
                                 bias=1.0).then_inc(cact.h, 1)
                cact.v += 3
                @blk.vector
                def _(v, ntok=ntok, wait=cact.v, v0=cdve.v):
                    v.wait_ge(cact.h, wait)
                    v.tensor_scalar(dt_sp[:, 0:ntok], dt_st[:, 0:ntok], 0.0, None,
                                    ALU.max).then_inc(cdve.h, 1)
                    v.wait_ge(cdve.h, v0 + 1)
                    v.tensor_tensor(dt_sp[:, 0:ntok], dt_sp[:, 0:ntok], sp_t1[:, 0:ntok],
                                    ALU.add).then_inc(cdve.h, 1)
                cdve.v += 2
                tgt = sp_dtr if is_rec else sp_dto
                dev_ = pa.event(1)
                @blk.scalar
                def _(s, tgt=tgt, wofs=wofs, ntok=ntok, wait=cdve.v, dev_=dev_):
                    s.wait_ge(cdve.h, wait)
                    s.dma_start(out=tgt.ap()[:, wofs:wofs + ntok],
                                in_=dt_sp[:, 0:ntok]).then_inc(dev_[0], 16)
                m["dt_free"] = dev_
            elif is_hbc:
                bb = jb - 64
                @blk.scalar
                def _(s, nt5=nt5, ntok=ntok, bank0=bank0, wait=pe_main,
                      cw=m["conv_free"], tw=m["tails_w"]):
                    s.wait_ge(cpe.h, wait)
                    w = max(cw, tw)
                    if w:
                        s.wait_ge(cdve.h, w)
                    for t5 in range(nt5):
                        n = min(512, ntok - t5 * 512)
                        s.activation(conv_in[:, 3 + t5 * 512:3 + t5 * 512 + n],
                                     ps[bank0 + t5][:, 0:n], AF.Copy).then_inc(cact.h, 1)
                cact.v += nt5
                m["bank_free"][bank0] = cact.v
                conv_emit(bb, wofs, ntok, is_rec, cact.v)
            else:
                hb = ji % 2
                @blk.scalar
                def _(s, nt5=nt5, ntok=ntok, bank0=bank0, hb=hb, wait=pe_main,
                      gw=m["gev_free"][hb]):
                    s.wait_ge(cpe.h, wait)
                    wge(s, gw)
                    for t5 in range(nt5):
                        n = min(512, ntok - t5 * 512)
                        s.activation(gev[:, hb, t5 * 512:t5 * 512 + n],
                                     ps[bank0 + t5][:, 0:n], AF.Copy).then_inc(cact.h, 1)
                cact.v += nt5
                m["bank_free"][bank0] = cact.v
                gv = pa.event(1)
                @blk.scalar
                def _(s, jb=jb, hb=hb, wofs=wofs, ntok=ntok, wait=cact.v, gv=gv):
                    s.wait_ge(cact.h, wait)
                    s.dma_start(out=sp_gate.ap()[jb * 128:(jb + 1) * 128, wofs:wofs + ntok],
                                in_=gev[:, hb, 0:ntok]).then_inc(gv[0], 16)
                m["gev_free"][hb] = gv
        m["xnT_free"] = cpe.v

    rec_widths = ([1024] * 6 + [128]) if dev_rec is None else dev_rec
    wofs = 0
    for wi, wn in enumerate(rec_widths):
        emit_ln(rec_x, wofs, wn, xnr_scr, 0)
        emit_xnT(xnr_scr, 0, wn, xnT)
        emit_window(True, wofs, wn, False)
        wofs += wn

    @blk.vector
    def _(v):
        v.memset(xn_out[0:13, 0, :], 0.0).then_inc(cdve.h, 1)
    cdve.v += 1
    zev = pa.event(1)
    @blk.scalar
    def _(s, wait=cdve.v, zev=zev):
        s.wait_ge(cdve.h, wait)
        s.dma_start(out=halo_scr.ap()[0:13, :], in_=xn_out[0:13, 0, :]).then_inc(zev[0], 16)
    m["xn_free"][0] = zev
    emit_ln(own_x, 0, 3, halo_scr, 0, halo=True)
    emit_xnT(halo_scr, 0, 16, xnT_halo)
    halo_ev = m["xnT_ok"]
    for wi in range(2 if dev_own is None else dev_own):
        emit_ln(own_x, 3 + wi * WTOK, WTOK, xn_scr, wi * WTOK)
        emit_xnT(xn_scr, wi * WTOK, WTOK, xnT)
        emit_window(False, wi * WTOK, WTOK, wi == 0, halo_ev=halo_ev)

    def join_all():
        @blk.scalar
        def _(s, evs=tuple(pa.totals())):
            for ev in evs:
                s.wait_ge(*ev)
            s.nop().then_inc(chs.h, 1)
        chs.v += 1
        @blk.sync
        def _(s, evs=tuple(psy.totals())):
            for ev in evs:
                s.wait_ge(*ev)
            s.nop().then_inc(chs.h, 1)
        chs.v += 1
        @blk.gpsimd
        def _(g, evs=tuple(pg.totals()), w3=cdve.v, w4=cact.v, w5=cpe.v, wc=chs.v):
            for ev in evs:
                g.wait_ge(*ev)
            g.wait_ge(cdve.h, w3)
            g.wait_ge(cact.h, w4)
            g.wait_ge(cpe.h, w5)
            g.wait_ge(chs.h, wc)
            g.nop().then_inc(chs.h, 1)
        chs.v += 1
        return chs.v

    if stop == "bwin":
        dumps = [("xn_scr", xn_scr, [T_OWN, HID], BF16),
                 ("sp_hs", sp_hs, [INTER, T_OWN], BF16),
                 ("sp_B", sp_B, [G * NST, T_OWN], BF16),
                 ("sp_C", sp_C, [G * NST, T_OWN], BF16),
                 ("sp_gate", sp_gate, [INTER, T_OWN], BF16),
                 ("sp_dto", sp_dto, [H, T_OWN], F32),
                 ("sp_hsr", sp_hsr, [INTER, T_REC], BF16),
                 ("sp_Br", sp_Br, [G * NST, T_REC], BF16),
                 ("sp_dtr", sp_dtr, [H, T_REC], F32)]
        douts = {nm: nc.dram_tensor("o_" + nm, shape, dt, kind="ExternalOutput").ap()
                 for nm, _, shape, dt in dumps}
        join_all()
        dev2 = pg.event(len(dumps))
        @blk.gpsimd
        def _(g, dev2=dev2):
            for nm, srct, shape, dt in dumps:
                g.dma_start(out=douts[nm], in_=srct.ap()).then_inc(dev2[0], 16)
            g.wait_ge(*dev2)
        bst.close()
        st.close()
        return b

    jv_b = join_all()
    bst.close()

    sst = ExitStack()
    s_true = b.sb("s_true", [128, INTER], F32, sst)
    acum = b.sb("acum", [128, 1], F32, sst)
    xt = b.sb("xt", [128, INTER], BF16, sst)
    dtl = b.sb("dtl", [128, CH], F32, sst)
    adt = b.sb("adt", [128, CH], F32, sst)
    acs = b.sb("acs", [128, CH], F32, sst)
    dtt_bf = b.sb("dtt_bf", [128, CH], BF16, sst)
    eacol = b.sb("eacol", [128, 1], F32, sst)
    earow = b.sb("earow", [1, 128], F32, sst)
    eah = b.sb("eah", [128, 128], F32, sst)
    tbuf = b.sb("tbuf", [128, 512], F32, sst)
    d0st = ExitStack()
    bt = b.sb("bt", [128, G * NST], BF16, d0st)
    zbuf = b.sb("zbuf", [128, INTER], BF16, d0st)
    dec_sb = b.sb("dec_sb", [128, CH], F32, d0st)
    dect_bf = b.sb("dect_bf", [128, CH], BF16, d0st)

    for eng in ("vector", "scalar", "tensor", "sync", "gpsimd"):
        @getattr(blk, eng)
        def _(e, w=jv_b):
            e.wait_ge(chs.h, w)

    @blk.vector
    def _(v):
        v.memset(s_true[:], 0.0)
        v.memset(acum[:], 0.0).then_inc(cdve.h, 1)
    cdve.v += 1

    def emit_d0(is_rec, nch, src_hs, src_B, src_dt, spill_z):
        for c in range(nch):
            dtev = pa.event(1)
            @blk.scalar
            def _(s, c=c, dtev=dtev, src_dt=src_dt, wpe=cpe.v, wdve=cdve.v):
                s.wait_ge(cpe.h, wpe)
                s.wait_ge(cdve.h, wdve)
                s.dma_start(out=dtl[:], in_=src_dt.ap()[:, c * CH:(c + 1) * CH]
                            ).then_inc(dtev[0], 16)
            @blk.vector
            def _(v, c=c, dtev=dtev, is_rec=is_rec, v0=cdve.v):
                v.wait_ge(*dtev)
                k = v0
                if is_rec:
                    v.tensor_scalar(dtl[:], dtl[:], rmask_sb[:, c:c + 1], None,
                                    ALU.mult).then_inc(cdve.h, 1)
                    k += 1
                    v.wait_ge(cdve.h, k)
                v.tensor_scalar(adt[:], dtl[:], nega_sb[:, 0:1], None,
                                ALU.mult).then_inc(cdve.h, 1)
                k += 1
                v.wait_ge(cdve.h, k)
                v.tensor_tensor_scan(acs[:], adt[:], zeros_sb[:, 0:128], 0.0,
                                     ALU.add, ALU.add).then_inc(cdve.h, 1)
                k += 1
                v.wait_ge(cdve.h, k)
                v.tensor_tensor(acum[:], acum[:], acs[:, CH - 1:CH],
                                ALU.add).then_inc(cdve.h, 1)
            cdve.v += 4 if is_rec else 3
            dve_acs = cdve.v
            @blk.scalar
            def _(s, wait=dve_acs, a0=cact.v):
                s.wait_ge(cdve.h, wait)
                s.activation(dec_sb[:], acs[:], AF.Exp, scale=-1.0,
                             bias=acs[:, CH - 1:CH]).then_inc(cact.h, 1)
                s.activation(eacol[:], acs[:, CH - 1:CH], AF.Exp).then_inc(cact.h, 1)
            cact.v += 2
            act_dec = cact.v
            @blk.tensor
            def _(t, wait=act_dec, dve_w=dve_acs):
                t.wait_ge(cact.h, wait)
                t.transpose(ps_e[:, 0:128], dtl[:], idf_sb[:]).then_inc(cpe.h, 1)
            cpe.v += 1
            @blk.vector
            def _(v, wait=cpe.v):
                v.wait_ge(cpe.h, wait)
                v.tensor_copy(dtt_bf[:], ps_e[:, 0:128]).then_inc(cdve.h, 1)
            cdve.v += 1
            @blk.tensor
            def _(t, wait=cdve.v):
                t.wait_ge(cdve.h, wait)
                t.transpose(ps_e[:, 0:128], dec_sb[:], idf_sb[:]).then_inc(cpe.h, 1)
            cpe.v += 1
            @blk.vector
            def _(v, wait=cpe.v):
                v.wait_ge(cpe.h, wait)
                v.tensor_copy(dect_bf[:], ps_e[:, 0:128]).then_inc(cdve.h, 1)
            cdve.v += 1
            @blk.tensor
            def _(t, wait=cdve.v):
                t.wait_ge(cdve.h, wait)
                t.transpose(ps_e[0:1, 0:128], eacol[:], idf_sb[:]).then_inc(cpe.h, 1)
            cpe.v += 1
            @blk.vector
            def _(v, wait=cpe.v):
                v.wait_ge(cpe.h, wait)
                v.tensor_copy(earow[:], ps_e[0:1, 0:128]).then_inc(cdve.h, 1)
            cdve.v += 1
            @blk.tensor
            def _(t, wait=cdve.v):
                t.wait_ge(cdve.h, wait)
                t.matmul(ps_e[:, 0:128], ones_sb[:], earow[:], start=True,
                         stop=True).then_inc(cpe.h, 1)
            cpe.v += 1
            @blk.vector
            def _(v, wait=cpe.v):
                v.wait_ge(cpe.h, wait)
                v.tensor_copy(eah[:], ps_e[:, 0:128]).then_inc(cdve.h, 1)
            cdve.v += 1
            eah_ok = cdve.v
            xev = psy.event(1)
            @blk.sync
            def _(s, c=c, xev=xev, src_hs=src_hs, gate=cdve.v):
                s.wait_ge(cdve.h, gate)
                s.dma_start_transpose(xt[:], src_hs.ap()[:, c * CH:(c + 1) * CH]
                                      ).then_inc(xev[0], 16)
            bev = psy.event(1)
            @blk.sync
            def _(s, c=c, bev=bev, src_B=src_B):
                s.dma_start_transpose(bt[:], src_B.ap()[:, c * CH:(c + 1) * CH]
                                      ).then_inc(bev[0], 16)
            @blk.vector
            def _(v, xev=xev, v0=cdve.v):
                v.wait_ge(*xev)
                v.tensor_tensor(
                    xt[:].rearrange("p (h q) -> p h q", h=H),
                    xt[:].rearrange("p (h q) -> p h q", h=H),
                    dtt_bf[:].unsqueeze(2).to_broadcast([128, H, PD]),
                    ALU.mult).then_inc(cdve.h, 1)
                v.wait_ge(cdve.h, v0 + 1)
                v.tensor_tensor(
                    xt[:].rearrange("p (h q) -> p h q", h=H),
                    xt[:].rearrange("p (h q) -> p h q", h=H),
                    dect_bf[:].unsqueeze(2).to_broadcast([128, H, PD]),
                    ALU.mult).then_inc(cdve.h, 1)
            cdve.v += 2
            x2_ok = cdve.v
            for j in range(16):
                bank = ps[j % 2]
                @blk.tensor
                def _(t, j=j, bank=bank, bev=bev, xw=x2_ok,
                      bfree=m["bank_free"].get(("z", j % 2), 0)):
                    if j < 2:
                        t.wait_ge(*bev)
                        t.wait_ge(cdve.h, xw)
                    if bfree:
                        t.wait_ge(cdve.h, bfree)
                    for q in range(8):
                        h = 8 * j + q
                        ins = t.matmul(bank[:, q * PD:(q + 1) * PD],
                                       bt[:, q * NST:(q + 1) * NST],
                                       xt[:, h * PD:(h + 1) * PD],
                                       start=True, stop=True)
                        if q == 7:
                            ins.then_inc(cpe.h, 1)
                cpe.v += 1
                @blk.vector
                def _(v, j=j, bank=bank, wait=cpe.v, ew=eah_ok, v0=cdve.v,
                      spill_z=spill_z):
                    v.wait_ge(cpe.h, wait)
                    v.wait_ge(cdve.h, max(ew, v0))
                    sl = slice(j * 512, (j + 1) * 512)
                    v.tensor_tensor(
                        tbuf[:].rearrange("p (h q) -> p h q", h=8),
                        s_true[:, sl].rearrange("p (h q) -> p h q", h=8),
                        eah[:, j * 8:(j + 1) * 8].unsqueeze(2).to_broadcast([128, 8, PD]),
                        ALU.mult).then_inc(cdve.h, 1)
                    v.wait_ge(cdve.h, v0 + 1)
                    if spill_z:
                        v.tensor_copy(zbuf[:, sl], bank[:]).then_inc(cdve.h, 1)
                        v.wait_ge(cdve.h, v0 + 2)
                    v.tensor_tensor(s_true[:, sl], tbuf[:], bank[:],
                                    ALU.add).then_inc(cdve.h, 1)
                cdve.v += 3 if spill_z else 2
                m["bank_free"][("z", j % 2)] = cdve.v
            if spill_z:
                zev = pa.event(1)
                @blk.scalar
                def _(s, c=c, wait=cdve.v, zev=zev):
                    s.wait_ge(cdve.h, wait)
                    s.dma_start(out=sp_Z.ap()[c], in_=zbuf[:]).then_inc(zev[0], 16)
                m["z_spill"] = zev
                @blk.vector
                def _(v, zev=zev):
                    v.wait_ge(*zev)

    def emit_publish(slot_S, flag_row, slots=(0, 1)):
        dev_ = pg.event(2)
        fev = pg.event(1)
        @blk.gpsimd
        def _(g, w=cdve.v, dev_=dev_, fev=fev):
            g.wait_ge(cdve.h, w)
            rank = nc.partition_id()
            for r in g.Switch(rank, 8):
                s = r % 2
                if s in slots:
                    tS = slot_S(s)
                    g.dma_start(out=sh_S.ap()[tS], in_=s_true[:]).then_inc(dev_[0], 16)
                    g.dma_start(out=sh_A.ap()[tS], in_=acum[:]).then_inc(dev_[0], 16)
                    g.wait_ge(*dev_)
                    g.dma_start(out=sh_flags.ap()[flag_row(s):flag_row(s) + 1, :],
                                in_=non_sb[:]).then_inc(fev[0], 16)
                else:
                    g.dma_start(out=dum_scr.ap(), in_=s_true[:]).then_inc(dev_[0], 16)
                    g.dma_start(out=dum_scr2.ap(), in_=acum[:]).then_inc(dev_[0], 16)
                    g.wait_ge(*dev_)
                    g.dma_start(out=dum_scr3.ap(), in_=non_sb[:]).then_inc(fev[0], 16)
        return dev_

    def emit_poll(rows):
        if not sim_poll:
            @blk.scalar
            def _(s):
                s.nop().then_inc(chs.h, 1)
            chs.v += 1
            return chs.v
        pn = m["polln"] = m.get("polln", 0) + 1
        pdl = b.ctr(f"pdl{pn}")
        @blk.scalar
        def _(s, rows=tuple(rows), pn=pn, pdl=pdl):
            with (
                s.register(f"cont{pn}") as cont,
                s.register(f"f0_{pn}") as f0,
                s.register(f"f1_{pn}") as f1,
                s.register(f"n0_{pn}") as n0,
                s.register(f"n1_{pn}") as n1,
                s.register(f"eq0_{pn}") as eq0,
                s.register(f"both{pn}") as both,
                s.register(f"tgt{pn}") as tgt,
            ):
                s.reg_load(n0, non_sb[0:1, 0:1])
                s.reg_load(n1, non_sb[0:1, 1:2])
                s.reg_mov(tgt, 0)
                for fr in rows:
                    s.reg_mov(cont, 1)
                    with s.While(cont):
                        s.reg_alu(tgt, tgt, 16, ALU.add)
                        s.dma_start(out=flag_sb[:], in_=sh_flags.ap()[fr:fr + 1, :]
                                    ).then_inc(pdl.h, 16)
                        s.wait_ge(pdl.h, tgt)
                        s.reg_load(f0, flag_sb[0:1, 0:1])
                        s.reg_load(f1, flag_sb[0:1, 1:2])
                        s.reg_alu(eq0, f0, n0, ALU.is_equal)
                        s.reg_alu(both, f1, n1, ALU.is_equal)
                        s.reg_alu(both, both, eq0, ALU.bitwise_and)
                        s.reg_alu(cont, both, 1, ALU.bitwise_xor)
            s.nop().then_inc(chs.h, 1)
        chs.v += 1
        return chs.v

    rec_nch = sum(rec_widths) // CH
    own_nch = ((2 if dev_own is None else dev_own) * WTOK) // CH
    if rec_nch:
        emit_d0(True, rec_nch, sp_hsr, sp_Br, sp_dtr, False)
    pub_ev = emit_publish(lambda s: s, lambda s: 2 + s)
    @blk.vector
    def _(v, w=cdve.v, pub_ev=pub_ev):
        v.wait_ge(cdve.h, w)
        v.wait_ge(*pub_ev)
        v.memset(s_true[:], 0.0)
        v.memset(acum[:], 0.0).then_inc(cdve.h, 1)
    cdve.v += 1
    emit_d0(False, own_nch, sp_hs, sp_B, sp_dto, True)
    emit_publish(lambda s: 2, lambda s: 4, slots=(0,))
    d0st.close()

    if stop == "d0":
        join_all()
        sst.close()
        st.close()
        return b

    wst = ExitStack()
    wq_sb = b.sb("wq_sb", [128, 2, 512], U8, wst)
    t8a = b.sb("t8a", [128, 512], U8, wst)
    t8b = b.sb("t8b", [128, 512], U8, wst)
    fhi = b.sb("fhi", [128, 512], F32, wst)
    flo = b.sb("flo", [128, 512], F32, wst)
    dqh = b.sb("dqh", [128, 2, 512], BF16, wst)
    dql = b.sb("dql", [128, 2, 512], BF16, wst)
    zrow_sb = b.sb("zrow_sb", [4, 2, FFN_I // 2], F32, wst)
    srow_sb = b.sb("srow_sb", [4, 2, FFN_I // 2], F32, wst)
    ind4_sb = b.sb("ind4_sb", [4, 128], F32, wst)

    cev_w = pa.event(1)
    @blk.scalar
    def _(s, cev=cev_w):
        s.dma_start(out=ind4_sb[:], in_=ind4_i).then_inc(cev[0], 16)
    @blk.tensor
    def _(e, cev=cev_w):
        e.wait_ge(*cev)

    WCFG = [
        (gq_pk, g_zsT, 64, sp_wg, 16, [512] * 10 + [384]),
        (uq_pk, u_zsT, 64, sp_wu, 16, [512] * 10 + [384]),
        (dq_pk, d_zsT, 172, sp_wd, 43, [512] * 4),
    ]
    if dev_w is not None:
        WCFG = [(a, z, g_, d_, min(n, dev_w), o) for (a, z, g_, d_, n, o) in WCFG]
    w0_dve = cdve.v
    w0_pool = cpool.v
    gn = 0
    for wq_ap, zs_ap, GR, dst_scr, nkb, obs in WCFG:
        ocols = sum(obs)
        for kb in range(nkb):
            zev = pa.event(2)
            kslot = kb % 2
            @blk.scalar
            def _(s, kb=kb, kslot=kslot, zev=zev, zs_ap=zs_ap, GR=GR, oc=ocols,
                  zw=m.get(("zsf", kslot), 0)):
                if zw:
                    s.wait_ge(cpe.h, zw)
                s.dma_start(out=zrow_sb[:, kslot, 0:oc],
                            in_=zs_ap[4 * kb:4 * kb + 4]).then_inc(zev[0], 16)
                s.dma_start(out=srow_sb[:, kslot, 0:oc],
                            in_=zs_ap[GR + 4 * kb:GR + 4 * kb + 4]).then_inc(zev[0], 16)
            off = 0
            for W in obs:
                p2 = gn % 2
                wev = pa.event(1)
                @blk.scalar
                def _(s, kb=kb, p2=p2, off=off, W=W, wev=wev, wq_ap=wq_ap,
                      tw=m.get(("wqf", p2), 0)):
                    if tw:
                        s.wait_ge(cdve.h, tw)
                    s.dma_start(out=wq_sb[:, p2, 0:W],
                                in_=wq_ap[128 * kb:128 * (kb + 1), off:off + W]
                                ).then_inc(wev[0], 16)
                bz = ps[2 + 2 * p2]
                bs_ = ps[3 + 2 * p2]
                @blk.tensor
                def _(t, kslot=kslot, off=off, W=W, zev=zev, bz=bz, bs_=bs_,
                      bw=m.get(("wbank", p2), 0), w0=w0_dve):
                    t.wait_ge(*zev)
                    if bw:
                        t.wait_ge(cdve.h, bw)
                    else:
                        t.wait_ge(cdve.h, w0)
                    t.matmul(bz[:, 0:W], ind4_sb[:], zrow_sb[:, kslot, off:off + W],
                             start=True, stop=True).then_inc(cpe.h, 1)
                    t.matmul(bs_[:, 0:W], ind4_sb[:], srow_sb[:, kslot, off:off + W],
                             start=True, stop=True).then_inc(cpe.h, 1)
                cpe.v += 2
                m[("zsf", kslot)] = cpe.v
                ndma = 2 if dst_scr is not sp_wd else 1
                oevh = pg.event(ndma)
                oevl = pg.event(ndma)
                @blk.vector
                def _(v, p2=p2, W=W, wev=wev, bz=bz, bs_=bs_, pw=cpe.v,
                      owh=m.get(("dqfh", p2), None), owl=m.get(("dqfl", p2), None),
                      v0=cdve.v):
                    v.wait_ge(*wev)
                    v.tensor_scalar(t8a[:, 0:W], wq_sb[:, p2, 0:W], 4, 15,
                                    ALU.logical_shift_right,
                                    ALU.bitwise_and).then_inc(cdve.h, 1)
                    v.tensor_scalar(t8b[:, 0:W], wq_sb[:, p2, 0:W], 15, None,
                                    ALU.bitwise_and).then_inc(cdve.h, 1)
                    v.wait_ge(cdve.h, v0 + 1)
                    v.tensor_copy(fhi[:, 0:W], t8a[:, 0:W]).then_inc(cdve.h, 1)
                    v.wait_ge(cdve.h, v0 + 2)
                    v.tensor_copy(flo[:, 0:W], t8b[:, 0:W]).then_inc(cdve.h, 1)
                    v.wait_ge(cdve.h, v0 + 4)
                    v.wait_ge(cpe.h, pw)
                    v.tensor_tensor(fhi[:, 0:W], fhi[:, 0:W], bz[:, 0:W],
                                    ALU.subtract).then_inc(cdve.h, 1)
                    v.tensor_tensor(flo[:, 0:W], flo[:, 0:W], bz[:, 0:W],
                                    ALU.subtract).then_inc(cdve.h, 1)
                    v.wait_ge(cdve.h, v0 + 5)
                    if owh is not None:
                        v.wait_ge(*owh)
                    v.tensor_tensor(dqh[:, p2, 0:W], fhi[:, 0:W], bs_[:, 0:W],
                                    ALU.mult).then_inc(cdve.h, 1)
                    v.wait_ge(cdve.h, v0 + 6)
                    if owl is not None:
                        v.wait_ge(*owl)
                    v.tensor_tensor(dql[:, p2, 0:W], flo[:, 0:W], bs_[:, 0:W],
                                    ALU.mult).then_inc(cdve.h, 1)
                cdve.v += 8
                m[("wqf", p2)] = cdve.v - 6
                m[("wbank", p2)] = cdve.v
                tiled = dst_scr is not sp_wd
                @blk.gpsimd
                def _(g, p2=p2, W=W, w=cdve.v, oevh=oevh, oevl=oevl,
                      dst_scr=dst_scr, kb=kb, off=off, tiled=tiled):
                    g.wait_ge(cdve.h, w)
                    if tiled:
                        tv = dst_scr.ap().rearrange(
                            "(pp two) fb (kcb kc2) ii -> two kcb kc2 pp fb ii",
                            two=2, kc2=2)
                        fb0, nfb = off // 128, W // 128
                        for a in range(2):
                            g.dma_start(
                                out=tv[0][kb][a][:, fb0:fb0 + nfb, :],
                                in_=dqh[a * 64:(a + 1) * 64, p2, 0:W]
                                ).then_inc(oevh[0], 16)
                            g.dma_start(
                                out=tv[1][kb][a][:, fb0:fb0 + nfb, :],
                                in_=dql[a * 64:(a + 1) * 64, p2, 0:W]
                                ).then_inc(oevl[0], 16)
                    else:
                        dvw = dst_scr.ap().rearrange("(kk two) o -> two kk o", two=2)
                        g.dma_start(out=dvw[0][128 * kb:128 * (kb + 1), off:off + W],
                                    in_=dqh[:, p2, 0:W]).then_inc(oevh[0], 16)
                        g.dma_start(out=dvw[1][128 * kb:128 * (kb + 1), off:off + W],
                                    in_=dql[:, p2, 0:W]).then_inc(oevl[0], 16)
                m[("dqfh", p2)] = oevh
                m[("dqfl", p2)] = oevl
                off += W
                gn += 1

    wtotals = tuple(pg.totals())
    @blk.gpsimd
    def _(g, evs=wtotals):
        for ev in evs:
            g.wait_ge(*ev)
        g.nop().then_inc(chs.h, 1)
    chs.v += 1
    wfence = chs.v
    for eng in ("vector", "scalar", "tensor", "sync"):
        @getattr(blk, eng)
        def _(e, w=wfence):
            e.wait_ge(chs.h, w)
    wc_ev = pg.event(3)
    wf_ev = pg.event(1)
    @blk.gpsimd
    def _(g, wc_ev=wc_ev, wf_ev=wf_ev):
        rank = nc.partition_id()
        hw = FFN_I // 2
        hw2 = HID // 2
        for r in g.Switch(rank, 8):
            s_ = r % 2
            g.dma_start(out=sh_wg.ap()[:, s_ * 43:(s_ + 1) * 43],
                        in_=sp_wg.ap()).then_inc(wc_ev[0], 16)
            g.dma_start(out=sh_wu.ap()[:, s_ * 43:(s_ + 1) * 43],
                        in_=sp_wu.ap()).then_inc(wc_ev[0], 16)
            g.dma_start(out=sh_wd.ap()[:, s_ * hw2:(s_ + 1) * hw2],
                        in_=sp_wd.ap()).then_inc(wc_ev[0], 16)
            g.wait_ge(*wc_ev)
            g.dma_start(out=sh_flags.ap()[s_:s_ + 1, :],
                        in_=non_sb[:]).then_inc(wf_ev[0], 16)
    wst.close()

    if stop == "wphase":
        join_all()
        sst.close()
        st.close()
        return b

    pw = emit_poll([2, 3, 4])
    for eng in ("vector", "scalar", "tensor", "gpsimd"):
        @getattr(blk, eng)
        def _(e, w=pw):
            e.wait_ge(chs.h, w)

    est = ExitStack()
    shs_sl = b.sb("shs_sl", [128, 2, 512], F32, est)
    acol = b.sb("acol", [128, 1], F32, est)

    @blk.vector
    def _(v, w=cdve.v):
        v.wait_ge(cdve.h, w)
        v.memset(s_true[:], 0.0).then_inc(cdve.h, 1)
    cdve.v += 1

    for j in range(3):
        aev = pa.event(1)
        @blk.scalar
        def _(s, j=j, aev=aev):
            s.dma_start(out=acol[:], in_=sh_A.ap()[j]).then_inc(aev[0], 16)
        @blk.vector
        def _(v, j=j, aev=aev):
            v.wait_ge(*aev)
            v.tensor_scalar(acol[:], acol[:], use_sb[:, j:j + 1], None,
                            ALU.mult).then_inc(cdve.h, 1)
        cdve.v += 1
        @blk.scalar
        def _(s, wait=cdve.v):
            s.wait_ge(cdve.h, wait)
            s.activation(eacol[:], acol[:], AF.Exp).then_inc(cact.h, 1)
        cact.v += 1
        @blk.tensor
        def _(t, wait=cact.v):
            t.wait_ge(cact.h, wait)
            t.transpose(ps_e[0:1, 0:128], eacol[:], idf_sb[:]).then_inc(cpe.h, 1)
        cpe.v += 1
        @blk.vector
        def _(v, wait=cpe.v):
            v.wait_ge(cpe.h, wait)
            v.tensor_copy(earow[:], ps_e[0:1, 0:128]).then_inc(cdve.h, 1)
        cdve.v += 1
        @blk.tensor
        def _(t, wait=cdve.v):
            t.wait_ge(cdve.h, wait)
            t.matmul(ps_e[:, 0:128], ones_sb[:], earow[:], start=True,
                     stop=True).then_inc(cpe.h, 1)
        cpe.v += 1
        @blk.vector
        def _(v, wait=cpe.v):
            v.wait_ge(cpe.h, wait)
            v.tensor_copy(eah[:], ps_e[:, 0:128]).then_inc(cdve.h, 1)
        cdve.v += 1
        eah_ok2 = cdve.v
        for sl in range(16):
            sb_ = sl % 2
            sev = pa.event(1)
            @blk.scalar
            def _(s, j=j, sl=sl, sb_=sb_, sev=sev, w=cdve.v):
                if sl >= 2:
                    s.wait_ge(cdve.h, w)
                s.dma_start(out=shs_sl[:, sb_, :],
                            in_=sh_S.ap()[j][:, sl * 512:(sl + 1) * 512]
                            ).then_inc(sev[0], 16)
            @blk.vector
            def _(v, j=j, sl=sl, sb_=sb_, sev=sev, ew=eah_ok2, v0=cdve.v):
                v.wait_ge(*sev)
                v.wait_ge(cdve.h, max(ew, v0))
                s2 = slice(sl * 512, (sl + 1) * 512)
                v.tensor_tensor(
                    tbuf[:].rearrange("p (h q) -> p h q", h=8),
                    s_true[:, s2].rearrange("p (h q) -> p h q", h=8),
                    eah[:, sl * 8:(sl + 1) * 8].unsqueeze(2).to_broadcast([128, 8, PD]),
                    ALU.mult).then_inc(cdve.h, 1)
                v.wait_ge(cdve.h, v0 + 1)
                v.scalar_tensor_tensor(s_true[:, s2], shs_sl[:, sb_, :],
                                       use_sb[:, j:j + 1], tbuf[:],
                                       ALU.mult, ALU.add).then_inc(cdve.h, 1)
            cdve.v += 2
    est.close()

    if stop == "ecomb":
        join_all()
        sst.close()
        st.close()
        return b

    dst_ = ExitStack()
    prev_bf = b.sb("prev_bf", [128, INTER], BF16, dst_)
    y_sb = b.sb("y_sb", [128, INTER], F32, dst_)
    gate_c = b.sb("gate_c", [128, 64, CH], BF16, dst_)
    hs_c = b.sb("hs_c", [128, 64, CH], BF16, dst_)
    zrd = b.sb("zrd", [128, INTER], BF16, dst_)
    ball = b.sb("ball", [128, G, NST], BF16, dst_)
    call = b.sb("call", [128, G, NST], BF16, dst_)
    acsT = b.sb("acsT", [128, CH], F32, dst_)
    arow = b.sb("arow", [1, 4096], F32, dst_)
    gm_sb = b.sb("gm_sb", [128, 1024], BF16, dst_)
    eb_sb = b.sb("eb_sb", [128, 1024], BF16, dst_)
    t2_sb = b.sb("t2_sb", [128, 1024], F32, dst_)
    r8_bf = b.sb("r8_bf", [128, 1024], BF16, dst_)
    c2_bf = b.sb("c2_bf", [128, 1024], BF16, dst_)
    sgbuf = b.sb("sgbuf", [128, 1024], F32, dst_)
    vall = b.sb("vall", [1, 1024], F32, dst_)
    vr1 = b.sb("vr1", [1, 128], F32, dst_)
    vr2 = b.sb("vr2", [1, 128], F32, dst_)
    rsx = b.sb("rsx", [128, 1024], F32, dst_)
    onesP = b.sb("onesP", [128, 1], F32, dst_)

    @blk.vector
    def _(v):
        v.memset(onesP[:], 1.0).then_inc(cdve.h, 1)
    cdve.v += 1

    dnch = own_nch
    for c in range(dnch):
        dtev = pa.event(1)
        @blk.scalar
        def _(s, c=c, dtev=dtev, wpe=cpe.v, wdve=cdve.v):
            s.wait_ge(cpe.h, wpe)
            s.wait_ge(cdve.h, wdve)
            s.dma_start(out=dtl[:], in_=sp_dto.ap()[:, c * CH:(c + 1) * CH]
                        ).then_inc(dtev[0], 16)
        @blk.vector
        def _(v, dtev=dtev, v0=cdve.v):
            v.wait_ge(*dtev)
            v.tensor_scalar(adt[:], dtl[:], nega_sb[:, 0:1], None,
                            ALU.mult).then_inc(cdve.h, 1)
            v.wait_ge(cdve.h, v0 + 1)
            v.tensor_tensor_scan(acs[:], adt[:], zeros_sb[:, 0:128], 0.0,
                                 ALU.add, ALU.add).then_inc(cdve.h, 1)
        cdve.v += 2
        dve_acs = cdve.v
        @blk.scalar
        def _(s, wait=dve_acs):
            s.wait_ge(cdve.h, wait)
            s.activation(eacol[:], acs[:, CH - 1:CH], AF.Exp).then_inc(cact.h, 1)
        cact.v += 1
        @blk.tensor
        def _(t, wait=cact.v):
            t.wait_ge(cact.h, wait)
            t.transpose(ps_e[0:1, 0:128], eacol[:], idf_sb[:]).then_inc(cpe.h, 1)
        cpe.v += 1
        @blk.vector
        def _(v, wait=cpe.v):
            v.wait_ge(cpe.h, wait)
            v.tensor_copy(earow[:], ps_e[0:1, 0:128]).then_inc(cdve.h, 1)
        cdve.v += 1
        @blk.tensor
        def _(t, wait=cdve.v):
            t.wait_ge(cdve.h, wait)
            t.matmul(ps_e[:, 0:128], ones_sb[:], earow[:], start=True,
                     stop=True).then_inc(cpe.h, 1)
        cpe.v += 1
        @blk.vector
        def _(v, wait=cpe.v):
            v.wait_ge(cpe.h, wait)
            v.tensor_copy(eah[:], ps_e[:, 0:128]).then_inc(cdve.h, 1)
        cdve.v += 1
        eah_ok = cdve.v
        @blk.tensor
        def _(t, wait=cdve.v):
            t.wait_ge(cdve.h, wait)
            t.transpose(ps_e[:, 128:256], acs[:], idf_sb[:]).then_inc(cpe.h, 1)
        cpe.v += 1
        @blk.vector
        def _(v, wait=cpe.v):
            v.wait_ge(cpe.h, wait)
            v.tensor_copy(acsT[:], ps_e[:, 128:256]).then_inc(cdve.h, 1)
        cdve.v += 1
        a1 = pa.event(1)
        @blk.scalar
        def _(s, wait=dve_acs, a1=a1, pw=m.get("afl_free", 0)):
            s.wait_ge(cdve.h, wait)
            if pw:
                s.wait_ge(cpe.h, pw)
            s.dma_start(out=sp_aflat.ap(), in_=acs[:]).then_inc(a1[0], 16)
        @blk.vector
        def _(v, w=cdve.v):
            v.wait_ge(cdve.h, w)
            v.tensor_copy(prev_bf[:], s_true[:]).then_inc(cdve.h, 1)
        cdve.v += 1
        prev_ok = cdve.v
        bce = pa.event(2)
        gse = pa.event(2)
        ze = pa.event(1)
        @blk.scalar
        def _(s, c=c, bce=bce, gse=gse, ze=ze):
            s.dma_start(out=ball[:], in_=sp_B.ap()
                        .rearrange("(g n) t -> n g t", n=128)[:, :, c * CH:(c + 1) * CH]
                        ).then_inc(bce[0], 16)
            s.dma_start(out=call[:], in_=sp_C.ap()
                        .rearrange("(g n) t -> n g t", n=128)[:, :, c * CH:(c + 1) * CH]
                        ).then_inc(bce[0], 16)
            s.dma_start(out=gate_c[:], in_=sp_gate.ap()
                        .rearrange("(jb p) t -> p jb t", p=128)[:, :, c * CH:(c + 1) * CH]
                        ).then_inc(gse[0], 16)
            s.dma_start(out=hs_c[:], in_=sp_hs.ap()
                        .rearrange("(jb p) t -> p jb t", p=128)[:, :, c * CH:(c + 1) * CH]
                        ).then_inc(gse[0], 16)
            s.dma_start(out=zrd[:], in_=sp_Z.ap()[c]).then_inc(ze[0], 16)
        xev = psy.event(1)
        @blk.sync
        def _(s, c=c, xev=xev, gate=cdve.v):
            s.wait_ge(cdve.h, gate)
            s.dma_start_transpose(xt[:], sp_hs.ap()[:, c * CH:(c + 1) * CH]
                                  ).then_inc(xev[0], 16)
        @blk.tensor
        def _(t, wait=cdve.v, dtev=dtev):
            t.wait_ge(cdve.h, wait)
            t.transpose(ps_e[:, 256:384], dtl[:], idf_sb[:]).then_inc(cpe.h, 1)
        cpe.v += 1
        @blk.vector
        def _(v, wait=cpe.v):
            v.wait_ge(cpe.h, wait)
            v.tensor_copy(dtt_bf[:], ps_e[:, 256:384]).then_inc(cdve.h, 1)
        cdve.v += 1
        @blk.vector
        def _(v, xev=xev, v0=cdve.v):
            v.wait_ge(*xev)
            v.wait_ge(cdve.h, v0)
            v.tensor_tensor(
                xt[:].rearrange("p (h q) -> p h q", h=H),
                xt[:].rearrange("p (h q) -> p h q", h=H),
                dtt_bf[:].unsqueeze(2).to_broadcast([128, H, PD]),
                ALU.mult).then_inc(cdve.h, 1)
        cdve.v += 1
        x_ok = cdve.v
        @blk.tensor
        def _(t, bce=bce, gfree=m["bank_free"].get("gm", 0)):
            t.wait_ge(*bce)
            if gfree:
                t.wait_ge(cdve.h, gfree)
            for g2 in range(G):
                bank = ps[2] if g2 < 4 else ps[3]
                ins = t.matmul(bank[:, (g2 % 4) * 128:(g2 % 4 + 1) * 128],
                               ball[:, g2, :], call[:, g2, :], start=True, stop=True)
                if g2 in (3, 7):
                    ins.then_inc(cpe.h, 1)
        cpe.v += 2
        @blk.vector
        def _(v, wait=cpe.v, v0=cdve.v):
            v.wait_ge(cpe.h, wait)
            v.tensor_tensor(
                gm_sb[:, 0:512].rearrange("p (a l) -> p a l", a=4),
                ps[2][:].rearrange("p (a l) -> p a l", a=4),
                tri_sb[:].unsqueeze(1).to_broadcast([128, 4, 128]),
                ALU.mult).then_inc(cdve.h, 1)
            v.wait_ge(cdve.h, v0 + 1)
            v.tensor_tensor(
                gm_sb[:, 512:1024].rearrange("p (a l) -> p a l", a=4),
                ps[3][:].rearrange("p (a l) -> p a l", a=4),
                tri_sb[:].unsqueeze(1).to_broadcast([128, 4, 128]),
                ALU.mult).then_inc(cdve.h, 1)
        cdve.v += 2
        m["bank_free"]["gm"] = cdve.v
        gm_ok = cdve.v

        for bb in range(16):
            rbw = m["bank_free"].get("rb", 0)
            af = m.get(("arow", bb % 4), 0)
            rev = psy.event(1)
            @blk.sync
            def _(s, bb=bb, a1=a1, af=af, rev=rev):
                s.wait_ge(*a1)
                if af:
                    s.wait_ge(cpe.h, af)
                s.dma_start(out=arow[0:1, (bb % 4) * 1024:(bb % 4 + 1) * 1024],
                            in_=sp_aflat.ap().rearrange("(r q) l -> r (q l)", r=16)
                            [bb:bb + 1, :]).then_inc(rev[0], 16)
            @blk.tensor
            def _(t, bb=bb, rbw=rbw, rev=rev):
                t.wait_ge(*rev)
                if rbw:
                    t.wait_ge(cdve.h, rbw)
                off = (bb % 4) * 1024
                t.matmul(ps[4][:], ones_sb[:], arow[0:1, off:off + 512],
                         start=True, stop=True).then_inc(cpe.h, 1)
                t.matmul(ps[5][:], ones_sb[:], arow[0:1, off + 512:off + 1024],
                         start=True, stop=True).then_inc(cpe.h, 1)
            cpe.v += 2
            rb_pe = cpe.v
            m[("arow", bb % 4)] = cpe.v
            @blk.scalar
            def _(s, wait=rb_pe, a0=cact.v):
                s.wait_ge(cpe.h, wait)
                s.activation(eb_sb[:, 0:512], ps[4][:], AF.Exp).then_inc(cact.h, 1)
                s.activation(eb_sb[:, 512:1024], ps[5][:], AF.Exp).then_inc(cact.h, 1)
            cact.v += 2
            @blk.vector
            def _(v, bb=bb, wait=rb_pe, aw=cact.v, v0=cdve.v):
                v.wait_ge(cpe.h, wait)
                v.wait_ge(cact.h, aw)
                v.tensor_tensor(
                    t2_sb[:, 0:512].rearrange("p (a l) -> p a l", a=4),
                    ps[4][:].rearrange("p (a l) -> p a l", a=4),
                    acsT[:, bb * 8:bb * 8 + 4].unsqueeze(2).to_broadcast([128, 4, 128]),
                    ALU.subtract).then_inc(cdve.h, 1)
                v.wait_ge(cdve.h, v0 + 1)
                v.tensor_tensor(
                    t2_sb[:, 512:1024].rearrange("p (a l) -> p a l", a=4),
                    ps[5][:].rearrange("p (a l) -> p a l", a=4),
                    acsT[:, bb * 8 + 4:bb * 8 + 8].unsqueeze(2).to_broadcast([128, 4, 128]),
                    ALU.subtract).then_inc(cdve.h, 1)
                v.wait_ge(cdve.h, v0 + 2)
                v.tensor_scalar(t2_sb[:], t2_sb[:], 0.0, None,
                                ALU.min).then_inc(cdve.h, 1)
            cdve.v += 3
            m["bank_free"]["rb"] = cdve.v
            @blk.scalar
            def _(s, wait=cdve.v):
                s.wait_ge(cdve.h, wait)
                s.activation(t2_sb[:], t2_sb[:], AF.Exp).then_inc(cact.h, 1)
            cact.v += 1
            @blk.vector
            def _(v, wait=cact.v, gw=gm_ok, v0=cdve.v):
                v.wait_ge(cact.h, wait)
                v.wait_ge(cdve.h, gw)
                v.tensor_tensor(r8_bf[:], t2_sb[:], gm_sb[:], ALU.mult).then_inc(cdve.h, 1)
                v.wait_ge(cdve.h, v0 + 1)
                v.tensor_tensor(c2_bf[:], eb_sb[:], call[:].rearrange("p g n -> p (g n)"),
                                ALU.mult).then_inc(cdve.h, 1)
            cdve.v += 2
            rc_ok = cdve.v
            ybank = ps[bb % 2]
            yfree = m["bank_free"].get(("y", bb % 2), 0)
            @blk.tensor
            def _(t, bb=bb, ybank=ybank, wait=rc_ok, xw=x_ok, pw=prev_ok, yfree=yfree):
                t.wait_ge(cdve.h, max(wait, xw, pw))
                if yfree:
                    t.wait_ge(cdve.h, yfree)
                for pr in range(4):
                    h0 = 8 * bb + 2 * pr
                    col = slice(pr * 128, (pr + 1) * 128)
                    c0 = slice((2 * pr) * 128, (2 * pr + 1) * 128)
                    c1 = slice((2 * pr + 1) * 128, (2 * pr + 2) * 128)
                    t.matmul(ybank[0:64, col], xt[:, h0 * PD:(h0 + 1) * PD],
                             r8_bf[:, c0], start=True, stop=False)
                    t.matmul(ybank[0:64, col], prev_bf[:, h0 * PD:(h0 + 1) * PD],
                             c2_bf[:, c0], start=False, stop=True)
                    t.matmul(ybank[64:128, col], xt[:, (h0 + 1) * PD:(h0 + 2) * PD],
                             r8_bf[:, c1], start=True, stop=False)
                    ins = t.matmul(ybank[64:128, col], prev_bf[:, (h0 + 1) * PD:(h0 + 2) * PD],
                                   c2_bf[:, c1], start=False, stop=True)
                    if pr == 3:
                        ins.then_inc(cpe.h, 1)
            cpe.v += 1
            @blk.vector
            def _(v, bb=bb, ybank=ybank, wait=cpe.v, gse=gse, v0=cdve.v):
                v.wait_ge(cpe.h, wait)
                if bb == 0:
                    v.wait_ge(*gse)
                for pr in range(4):
                    jb = 4 * bb + pr
                    v.scalar_tensor_tensor(y_sb[:, jb * 128:(jb + 1) * 128],
                                           hs_c[:, jb, :], dcols_sb[:, jb:jb + 1],
                                           ybank[:, pr * 128:(pr + 1) * 128],
                                           ALU.mult, ALU.add).then_inc(cdve.h, 1)
            cdve.v += 4
            m["bank_free"][("y", bb % 2)] = cdve.v

        m["afl_free"] = cpe.v
        for sl in range(16):
            @blk.vector
            def _(v, sl=sl, ze=ze, ew=eah_ok, v0=cdve.v):
                if sl == 0:
                    v.wait_ge(*ze)
                v.wait_ge(cdve.h, max(ew, v0))
                s2 = slice(sl * 512, (sl + 1) * 512)
                v.tensor_tensor(
                    tbuf[:].rearrange("p (h q) -> p h q", h=8),
                    s_true[:, s2].rearrange("p (h q) -> p h q", h=8),
                    eah[:, sl * 8:(sl + 1) * 8].unsqueeze(2).to_broadcast([128, 8, PD]),
                    ALU.mult).then_inc(cdve.h, 1)
                v.wait_ge(cdve.h, v0 + 1)
                v.tensor_tensor(s_true[:, s2], tbuf[:], zrd[:, s2],
                                ALU.add).then_inc(cdve.h, 1)
            cdve.v += 2

        for q in range(8):
            @blk.scalar
            def _(s, q=q, w=cdve.v, a0=cact.v):
                s.wait_ge(cdve.h, w)
                s.activation(sgbuf[:], gate_c[:].rearrange("p jb t -> p (jb t)")
                             [:, q * 1024:(q + 1) * 1024], AF.Silu).then_inc(cact.h, 1)
            cact.v += 1
            @blk.vector
            def _(v, q=q, wait=cact.v, v0=cdve.v):
                v.wait_ge(cact.h, wait)
                v.wait_ge(cdve.h, v0)
                v.tensor_tensor(y_sb[:, q * 1024:(q + 1) * 1024],
                                y_sb[:, q * 1024:(q + 1) * 1024], sgbuf[:],
                                ALU.mult).then_inc(cdve.h, 1)
            cdve.v += 1

        for g2 in range(8):
            @blk.scalar
            def _(s, g2=g2, w=cdve.v):
                s.wait_ge(cdve.h, w)
                s.activation(sgbuf[:], y_sb[:, g2 * 1024:(g2 + 1) * 1024],
                             AF.Square).then_inc(cact.h, 1)
            cact.v += 1
            pfree = m["bank_free"].get("rms", 0)
            @blk.tensor
            def _(t, wait=cact.v, pfree=pfree):
                t.wait_ge(cact.h, wait)
                if pfree:
                    t.wait_ge(cdve.h, pfree)
                t.matmul(ps[4][0:1, :], onesP[:], sgbuf[:, 0:512], start=True,
                         stop=True).then_inc(cpe.h, 1)
                t.matmul(ps[5][0:1, :], onesP[:], sgbuf[:, 512:1024], start=True,
                         stop=True).then_inc(cpe.h, 1)
            cpe.v += 2
            @blk.vector
            def _(v, g2=g2, wait=cpe.v, v0=cdve.v):
                v.wait_ge(cpe.h, wait)
                v.tensor_reduce(vr1[:], ps[4][0:1, :].rearrange("a (jb l) -> a l jb", jb=4),
                                mybir.AxisListType.X, ALU.add).then_inc(cdve.h, 1)
                v.wait_ge(cdve.h, v0 + 1)
                v.tensor_reduce(vr2[:], ps[5][0:1, :].rearrange("a (jb l) -> a l jb", jb=4),
                                mybir.AxisListType.X, ALU.add).then_inc(cdve.h, 1)
                v.wait_ge(cdve.h, v0 + 2)
                v.tensor_tensor(vall[0:1, g2 * 128:(g2 + 1) * 128], vr1[:], vr2[:],
                                ALU.add).then_inc(cdve.h, 1)
            cdve.v += 3
            m["bank_free"]["rms"] = cdve.v
        @blk.scalar
        def _(s, w=cdve.v, a0=cact.v):
            s.wait_ge(cdve.h, w)
            s.activation(vall[:], vall[:], AF.Ln, scale=1.0 / 1024.0,
                         bias=eps_sb[0:1, :]).then_inc(cact.h, 1)
            s.wait_ge(cact.h, a0 + 1)
            s.activation(vall[:], vall[:], AF.Exp, scale=-0.5).then_inc(cact.h, 1)
        cact.v += 2
        @blk.tensor
        def _(t, wait=cact.v, pfree=m["bank_free"].get("rms", 0)):
            t.wait_ge(cact.h, wait)
            t.wait_ge(cdve.h, pfree)
            t.matmul(ps[4][:], ones_sb[:], vall[0:1, 0:512], start=True,
                     stop=True).then_inc(cpe.h, 1)
            t.matmul(ps[5][:], ones_sb[:], vall[0:1, 512:1024], start=True,
                     stop=True).then_inc(cpe.h, 1)
        cpe.v += 2
        @blk.vector
        def _(v, wait=cpe.v, v0=cdve.v):
            v.wait_ge(cpe.h, wait)
            v.tensor_copy(rsx[:, 0:512], ps[4][:]).then_inc(cdve.h, 1)
            v.wait_ge(cdve.h, v0 + 1)
            v.tensor_copy(rsx[:, 512:1024], ps[5][:]).then_inc(cdve.h, 1)
            v.wait_ge(cdve.h, v0 + 2)
            v.tensor_tensor(
                y_sb[:].rearrange("p (g d l) -> p g d l", g=8, d=8),
                y_sb[:].rearrange("p (g d l) -> p g d l", g=8, d=8),
                rsx[:].rearrange("p (g l) -> p g l", g=8).unsqueeze(2)
                .to_broadcast([128, 8, 8, 128]),
                ALU.mult).then_inc(cdve.h, 1)
            v.wait_ge(cdve.h, v0 + 3)
            v.tensor_tensor(
                y_sb[:].rearrange("p (jb t) -> p jb t", jb=64),
                y_sb[:].rearrange("p (jb t) -> p jb t", jb=64),
                nw_sb[:].unsqueeze(2).to_broadcast([128, 64, CH]),
                ALU.mult).then_inc(cdve.h, 1)
        cdve.v += 4
        m["bank_free"]["rms"] = cdve.v
        nev = pg.event(1)
        @blk.gpsimd
        def _(g, c=c, w=cdve.v, nev=nev):
            g.wait_ge(cdve.h, w)
            g.dma_start(out=sp_norm.ap().rearrange("(jb p) t -> p jb t", p=128)
                        [:, :, c * CH:(c + 1) * CH],
                        in_=y_sb[:].rearrange("p (jb t) -> p jb t", jb=64)
                        ).then_inc(nev[0], 16)
        m["nrm_spill"] = nev
        @blk.vector
        def _(v, nev=nev):
            v.wait_ge(*nev)

    dst_.close()

    if stop == "dscan":
        join_all()
        sst.close()
        st.close()
        return b

    join_all()
    sst.close()
    pwv = emit_poll([0, 1])
    for eng in ("vector", "scalar", "tensor", "gpsimd", "sync"):
        @getattr(blk, eng)
        def _(e, w=pwv):
            e.wait_ge(chs.h, w)

    cst = ExitStack()
    xnt_c = b.sb("xnt_c", [128, 32, WTOK], BF16, cst)
    wg_sb = b.sb("wg_sb", [128, 2, 32, 128], BF16, cst)
    wu_sb = b.sb("wu_sb", [128, 2, 32, 128], BF16, cst)
    sg_c = b.sb("sg_c", [128, 1024], F32, cst)
    h_bf = b.sb("h_bf", [128, 2, 1024], BF16, cst)

    n_th = 2 if dev_own is None else dev_own
    c0_dve = cdve.v
    for th in range(n_th):
        xev = psy.event(32)
        @blk.sync
        def _(s, th=th, xev=xev, pw=m.get("xnt_free", 0)):
            if pw:
                s.wait_ge(cpe.h, pw)
            for kc in range(32):
                s.dma_start_transpose(
                    xnt_c[:, kc, :],
                    xn_scr.ap()[th * WTOK:(th + 1) * WTOK, kc * 128:(kc + 1) * 128]
                    ).then_inc(xev[0], 16)
        for fb in range(86):
            p2 = fb % 2
            wgev = pa.event(1)
            wuev = pa.event(1)
            @blk.scalar
            def _(s, fb=fb, p2=p2, wgev=wgev, wuev=wuev,
                  fw=m.get(("cwf", p2), 0)):
                if fw:
                    s.wait_ge(cpe.h, fw)
                s.dma_start(out=wg_sb[:, p2],
                            in_=sh_wg.ap()[:, fb]).then_inc(wgev[0], 16)
                s.dma_start(out=wu_sb[:, p2],
                            in_=sh_wu.ap()[:, fb]).then_inc(wuev[0], 16)
            @blk.tensor
            def _(t, fb=fb, p2=p2, xev=xev, wgev=wgev, wuev=wuev,
                  aw=m.get("c_act", 0), pw=m.get("c_pool", 0), d0=c0_dve):
                t.wait_ge(*xev)
                t.wait_ge(*wgev)
                t.wait_ge(*wuev)
                if aw:
                    t.wait_ge(cact.h, aw)
                if pw:
                    t.wait_ge(cdve.h, pw)
                else:
                    t.wait_ge(cdve.h, d0)
                for tq in range(2):
                    for kc in range(32):
                        ins = t.matmul(ps[tq][:], wg_sb[:, p2, kc, :],
                                       xnt_c[:, kc, tq * 512:(tq + 1) * 512],
                                       start=(kc == 0), stop=(kc == 31))
                        if kc == 31:
                            ins.then_inc(cpe.h, 1)
                    for kc in range(32):
                        ins = t.matmul(ps[2 + tq][:], wu_sb[:, p2, kc, :],
                                       xnt_c[:, kc, tq * 512:(tq + 1) * 512],
                                       start=(kc == 0), stop=(kc == 31))
                        if kc == 31:
                            ins.then_inc(cpe.h, 1)
            cpe.v += 4
            m[("cwf", p2)] = cpe.v
            m["xnt_free"] = cpe.v
            @blk.scalar
            def _(s, wait=cpe.v, a0=cact.v):
                s.wait_ge(cpe.h, wait)
                s.activation(sg_c[:, 0:512], ps[0][:], AF.Silu).then_inc(cact.h, 1)
                s.activation(sg_c[:, 512:1024], ps[1][:], AF.Silu).then_inc(cact.h, 1)
            cact.v += 2
            m["c_act"] = cact.v
            hev = pg.event(1)
            @blk.vector
            def _(v, p2=p2, aw=cact.v, v0=cdve.v,
                  ow=m.get(("chf", p2), None)):
                v.wait_ge(cact.h, aw)
                if ow is not None:
                    v.wait_ge(*ow)
                v.tensor_tensor(h_bf[:, p2, 0:512], sg_c[:, 0:512], ps[2][:],
                                ALU.mult).then_inc(cdve.h, 1)
                v.wait_ge(cdve.h, v0 + 1)
                v.tensor_tensor(h_bf[:, p2, 512:1024], sg_c[:, 512:1024], ps[3][:],
                                ALU.mult).then_inc(cdve.h, 1)
            cdve.v += 2
            m["c_pool"] = cdve.v
            @blk.gpsimd
            def _(g, th=th, fb=fb, p2=p2, w=cdve.v, hev=hev):
                g.wait_ge(cdve.h, w)
                g.dma_start(out=sp_h.ap()[fb * 128:(fb + 1) * 128,
                                          th * WTOK:(th + 1) * WTOK],
                            in_=h_bf[:, p2]).then_inc(hev[0], 16)
            m[("chf", p2)] = hev
    cst.close()

    c2st = ExitStack()
    hblk_sb = b.sb("hblk_sb", [128, 86, 512], BF16, c2st)
    wd_sb = b.sb("wd_sb", [128, 2, 512], BF16, c2st)
    sdn_sb = b.sb("sdn_sb", [128, 2, 512], F32, c2st)

    n_tq = 4 if dev_own is None else 2 * dev_own
    c_end = (cpe.v, cdve.v, m[("chf", 0)], m.get(("chf", 1)))
    for tq in range(n_tq):
        hbev = pa.event(1)
        @blk.scalar
        def _(s, tq=tq, hbev=hbev, pw=m.get("hblk_free", 0), c_end=c_end):
            if pw:
                s.wait_ge(cpe.h, pw)
            else:
                s.wait_ge(cpe.h, c_end[0])
                s.wait_ge(cdve.h, c_end[1])
                s.wait_ge(*c_end[2])
                if c_end[3] is not None:
                    s.wait_ge(*c_end[3])
            s.dma_start(out=hblk_sb[:], in_=sp_h.ap()
                        .rearrange("(kb p) t -> p kb t", p=128)
                        [:, :, tq * 512:(tq + 1) * 512]).then_inc(hbev[0], 16)
        for db in range(8):
            for kb in range(86):
                p2 = kb % 2
                wdev = (pa if kb % 2 == 0 else pg).event(1)
                eng_ld = blk.scalar if kb % 2 == 0 else blk.gpsimd
                @eng_ld
                def _(s, db=db, kb=kb, p2=p2, wdev=wdev,
                      fw=m.get(("c2wf", p2), 0)):
                    if fw:
                        s.wait_ge(cpe.h, fw)
                    s.dma_start(out=wd_sb[:, p2], in_=sh_wd.ap()
                                [kb * 128:(kb + 1) * 128, db * 512:(db + 1) * 512]
                                ).then_inc(wdev[0], 16)
                @blk.tensor
                def _(t, db=db, kb=kb, p2=p2, wdev=wdev, hbev=hbev,
                      pw=m.get("c2_pool", 0)):
                    t.wait_ge(*wdev)
                    if kb == 0:
                        t.wait_ge(*hbev)
                        if pw:
                            t.wait_ge(cdve.h, pw)
                    for t4 in range(4):
                        ins = t.matmul(ps[t4][:], hblk_sb[:, kb, t4 * 128:(t4 + 1) * 128],
                                       wd_sb[:, p2], start=(kb == 0), stop=(kb == 85))
                        if t4 == 3:
                            ins.then_inc(cpe.h, 1)
                cpe.v += 1
                m[("c2wf", p2)] = cpe.v
            m["hblk_free"] = cpe.v
            for t4 in range(4):
                p2 = t4 % 2
                dnev = pg.event(1)
                @blk.vector
                def _(v, t4=t4, p2=p2, w=cpe.v,
                      ow=m.get(("dnf", p2), None)):
                    v.wait_ge(cpe.h, w)
                    if ow is not None:
                        v.wait_ge(*ow)
                    v.tensor_copy(sdn_sb[:, p2], ps[t4][:]).then_inc(cdve.h, 1)
                cdve.v += 1
                @blk.gpsimd
                def _(g, tq=tq, db=db, t4=t4, p2=p2, w=cdve.v, dnev=dnev):
                    g.wait_ge(cdve.h, w)
                    g.dma_start(out=sp_down.ap()
                                [tq * 512 + t4 * 128:tq * 512 + (t4 + 1) * 128,
                                 db * 512:(db + 1) * 512],
                                in_=sdn_sb[:, p2]).then_inc(dnev[0], 16)
                m[("dnf", p2)] = dnev
            m["c2_pool"] = cdve.v
    c2st.close()

    if stop == "ffn":
        join_all()
        st.close()
        return b

    fst = ExitStack()
    nt_res = b.sb("nt_res", [128, 64, 768], BF16, fst)
    wo_sb = b.sb("wo_sb", [128, 2, 512], BF16, fst)
    xres = b.sb("xres", [128, 6, 512], F32, fst)
    dres = b.sb("dres", [128, 6, 512], F32, fst)
    ostg = b.sb("ostg", [128, 2, 512], F32, fst)

    n_t128 = (T_OWN if dev_own is None else dev_own * WTOK) // 128
    tgs = []
    t0_ = 0
    while t0_ < n_t128:
        g_ = min(6, n_t128 - t0_)
        tgs.append((t0_, g_))
        t0_ += g_
    c2_end = (cpe.v, cdve.v, m[("dnf", 0)], m[("dnf", 1)])
    for gi, (tb0, ntg) in enumerate(tgs):
        ntev = pa.event(1)
        @blk.scalar
        def _(s, tb0=tb0, ntg=ntg, ntev=ntev, gi=gi,
              pw=m.get("nt_free", 0), c2_end=c2_end):
            if gi == 0:
                s.wait_ge(cpe.h, c2_end[0])
                s.wait_ge(cdve.h, c2_end[1])
                s.wait_ge(*c2_end[2])
                s.wait_ge(*c2_end[3])
            if pw:
                s.wait_ge(cpe.h, pw)
            s.dma_start(out=nt_res[:, :, 0:ntg * 128], in_=sp_norm.ap()
                        .rearrange("(jb p) t -> p jb t", p=128)
                        [:, :, tb0 * 128:(tb0 + ntg) * 128]).then_inc(ntev[0], 16)
        for db in range(8):
            xdev = pa.event(2)
            @blk.scalar
            def _(s, tb0=tb0, ntg=ntg, db=db, xdev=xdev,
                  vw=m.get("f_dve", 0)):
                if vw:
                    s.wait_ge(cdve.h, vw)
                s.dma_start(out=xres[:, 0:ntg, :], in_=own_x
                            [3 + tb0 * 128:3 + (tb0 + ntg) * 128,
                             db * 512:(db + 1) * 512]
                            .rearrange("(n p) d -> p n d", p=128)
                            ).then_inc(xdev[0], 16)
                s.dma_start(out=dres[:, 0:ntg, :], in_=sp_down.ap()
                            [tb0 * 128:(tb0 + ntg) * 128, db * 512:(db + 1) * 512]
                            .rearrange("(n p) d -> p n d", p=128)
                            ).then_inc(xdev[0], 16)
            for jb in range(64):
                p2 = jb % 2
                woev = pg.event(1)
                @blk.gpsimd
                def _(g, jb=jb, db=db, p2=p2, woev=woev,
                      fw=m.get(("fwf", p2), 0)):
                    if fw:
                        g.wait_ge(cpe.h, fw)
                    g.dma_start(out=wo_sb[:, p2], in_=w_outT
                                [jb * 128:(jb + 1) * 128, db * 512:(db + 1) * 512]
                                ).then_inc(woev[0], 16)
                @blk.tensor
                def _(t, jb=jb, ntg=ntg, p2=p2, woev=woev, ntev=ntev,
                      vw=m.get("f_dve", 0)):
                    t.wait_ge(*woev)
                    if jb == 0:
                        t.wait_ge(*ntev)
                        if vw:
                            t.wait_ge(cdve.h, vw)
                    for tt in range(ntg):
                        ins = t.matmul(ps[tt][:], nt_res[:, jb, tt * 128:(tt + 1) * 128],
                                       wo_sb[:, p2], start=(jb == 0), stop=(jb == 63))
                        if tt == ntg - 1:
                            ins.then_inc(cpe.h, 1)
                cpe.v += 1
                m[("fwf", p2)] = cpe.v
            m["nt_free"] = cpe.v
            for tt in range(ntg):
                p2 = tt % 2
                oev = pa.event(1)
                @blk.vector
                def _(v, tb0=tb0, tt=tt, db=db, p2=p2, w=cpe.v, xdev=xdev,
                      v0=cdve.v, ow=m.get(("onf", p2), None)):
                    v.wait_ge(cpe.h, w)
                    v.wait_ge(*xdev)
                    if ow is not None:
                        v.wait_ge(*ow)
                    v.tensor_tensor(ostg[:, p2], ps[tt][:], xres[:, tt, :],
                                    ALU.add).then_inc(cdve.h, 1)
                    v.wait_ge(cdve.h, v0 + 1)
                    v.tensor_tensor(ostg[:, p2], ostg[:, p2], dres[:, tt, :],
                                    ALU.add).then_inc(cdve.h, 1)
                cdve.v += 2
                @blk.scalar
                def _(s, tb0=tb0, tt=tt, db=db, p2=p2, w=cdve.v, oev=oev):
                    s.wait_ge(cdve.h, w)
                    s.dma_start(out=out[(tb0 + tt) * 128:(tb0 + tt + 1) * 128,
                                        db * 512:(db + 1) * 512],
                                in_=ostg[:, p2]).then_inc(oev[0], 16)
                m[("onf", p2)] = oev
            m["f_dve"] = cdve.v

    fst.close()
    join_all()
    st.close()
    return b


def _prep_host(inputs):
    x = np.ascontiguousarray(np.asarray(inputs["L0_attn_hidden_states"]).reshape(L, HID))
    f32 = np.float32
    w_in = np.asarray(inputs["L0_attn_in_proj_w"]).astype(f32)
    w_int = np.ascontiguousarray(
        w_in.reshape(PROJ // 128, 128, HID // 128, 128).transpose(0, 3, 2, 1))
    w_outT = np.ascontiguousarray(np.asarray(inputs["L0_attn_out_proj_w"]).T.astype(f32))
    conv_w = np.asarray(inputs["L0_attn_conv_w"]).reshape(CONV_DIM, KTAP)
    conv_w4 = np.ascontiguousarray(conv_w.reshape(80, 128, KTAP).transpose(1, 0, 2).astype(f32))
    conv_bc = np.ascontiguousarray(
        np.asarray(inputs["L0_attn_conv_b"]).reshape(80, 128).T.astype(f32))
    dtb_col = np.asarray(inputs["L0_attn_dt_bias"]).reshape(128, 1).astype(f32)
    alog_col = np.asarray(inputs["L0_attn_A_log"]).reshape(128, 1).astype(f32)
    dv = np.asarray(inputs["L0_attn_D"]).astype(f32)
    d_cols = np.ascontiguousarray(
        np.repeat(dv.reshape(64, 2), 64, axis=1).reshape(64, 128).T)
    nw = np.asarray(inputs["L0_attn_norm_weight"]).astype(f32)
    normw_cols = np.ascontiguousarray(nw.reshape(64, 128).T)
    trimask = np.ascontiguousarray(np.triu(np.ones((128, 128), f32)))
    ones_rw = np.ones((1, 128), f32)
    ident = np.eye(128).astype(ml_dtypes.bfloat16)
    ident_f = np.eye(128, dtype=f32)

    def lowb(a):
        a = np.ascontiguousarray(np.asarray(a).astype(np.int32))
        return a.view(np.uint8).reshape(a.shape[0], a.shape[1], 4)[:, :, 0]

    def zsT(z, s):
        return np.vstack([np.asarray(z).T.astype(f32), np.asarray(s).T.astype(f32)])

    gq_b = np.asarray(lowb(inputs["L0_ffn_gate_Wq"]).T)
    uq_b = np.asarray(lowb(inputs["L0_ffn_up_Wq"]).T)
    dq_b = np.asarray(lowb(inputs["L0_ffn_down_Wq"]).T)
    g_zsT_f = zsT(inputs["L0_ffn_gate_zero"], inputs["L0_ffn_gate_scale"])
    u_zsT_f = zsT(inputs["L0_ffn_up_zero"], inputs["L0_ffn_up_scale"])
    d_zsT_f = zsT(inputs["L0_ffn_down_zero"], inputs["L0_ffn_down_scale"])
    ind4v = np.zeros((4, 128), f32)
    for gg in range(4):
        ind4v[gg, gg * 32:(gg + 1) * 32] = 1.0

    rng = np.random.default_rng()
    nonce = rng.integers(1, 2 ** 30, (1, 2)).astype(np.int32)

    in_maps = []
    for r in range(NCORES):
        p, s = r // 2, r % 2
        start = T_OWN * r
        halo = x[start - 3:start] if start >= 3 else np.vstack(
            [np.zeros((3 - start, HID), f32), x[0:start]])
        ox = np.ascontiguousarray(np.vstack([halo, x[start:start + T_OWN]]))
        ln = 2048 * p
        rs = s * 2048 * p
        pad = T_REC - ln
        if rs >= pad:
            padreg = x[rs - pad:rs]
        else:
            padreg = np.vstack([np.zeros((pad - rs, HID), f32), x[0:rs]])
        rx = np.ascontiguousarray(np.vstack([padreg, x[rs:rs + ln]]))
        rmask = np.zeros((NCH_REC,), f32)
        rmask[pad // CH:] = 1.0
        rmask_cols = np.ascontiguousarray(np.broadcast_to(rmask, (128, NCH_REC)))
        umask = np.array([1.0, 1.0, 0.0 if s == 0 else 1.0], f32)
        use_colsv = np.ascontiguousarray(np.broadcast_to(umask, (128, 3)))
        hrow = FFN_I // 2
        hrow2 = HID // 2
        in_maps.append({
            "own_x": ox, "rec_x": rx, "rmask_cols": rmask_cols, "use_cols": use_colsv,
            "nonce": nonce, "w_int": w_int, "w_outT": w_outT,
            "conv_w4": conv_w4, "conv_bc": conv_bc, "dtb_col": dtb_col,
            "alog_col": alog_col, "d_cols": d_cols, "normw_cols": normw_cols,
            "trimask": trimask, "ones_row": ones_rw, "ident_bf": ident, "ident_f": ident_f,
            "gq_pk": np.ascontiguousarray(gq_b[:, s * hrow:(s + 1) * hrow]),
            "uq_pk": np.ascontiguousarray(uq_b[:, s * hrow:(s + 1) * hrow]),
            "dq_pk": np.ascontiguousarray(dq_b[:, s * hrow2:(s + 1) * hrow2]),
            "g_zsT": np.ascontiguousarray(g_zsT_f[:, s * hrow:(s + 1) * hrow]),
            "u_zsT": np.ascontiguousarray(u_zsT_f[:, s * hrow:(s + 1) * hrow]),
            "d_zsT": np.ascontiguousarray(d_zsT_f[:, s * hrow2:(s + 1) * hrow2]),
            "ind4_i": ind4v,
        })
    return in_maps


def kernel(**inputs):
    in_maps = _prep_host(inputs)
    bobj = build()
    res = run_bass_kernel_spmd(bobj.nc, in_maps, core_ids=list(range(NCORES)))
    outs = [res.results[r]["out"] for r in range(NCORES)]
    return np.concatenate(outs, axis=0).reshape(1, L, HID).astype(np.float32)


# version 05 (final)
# speedup vs baseline: 1.0407x; 1.0407x over previous
import sys
import numpy as np

for _p in ("/opt/trn_rl_repo",):
    if _p not in sys.path:
        sys.path.insert(0, _p)
import concourse.bass as bass
import concourse.mybir as mybir
from concourse.bass_utils import run_bass_kernel_spmd
import ml_dtypes
from contextlib import ExitStack

F32 = mybir.dt.float32
BF16 = mybir.dt.bfloat16
I32 = mybir.dt.int32
U8 = mybir.dt.uint8
AF = mybir.ActivationFunctionType
ALU = mybir.AluOpType

HID = 4096
H = 128
PD = 64
NST = 128
G = 8
KTAP = 4
CH = 128
INTER = 8192
CONV_DIM = 10240
PROJ = 18560
EPS = 1e-5
L = 16384
NCORES = 8
T_OWN = 2048
NCH_OWN = 16
T_REC = 6272
NCH_REC = 49
WTOK = 1024
KC = 32
J_DT = 144
FFN_I = 11008
NI = 86


class B:

    def __init__(self, stop=None, races=True):
        self.stop = stop
        self.nc = bass.Bass("TRN2", target_bir_lowering=False, num_devices=8,
                            detect_race_conditions=races)
        self.st = ExitStack()
        self.dbg = {}

    class Ctr:
        def __init__(self, st, nc, name):
            self.h = st.enter_context(nc.semaphore(name))
            self.v = 0

    def ctr(self, name):
        return B.Ctr(self.st, self.nc, name)

    class EvPool:
        def __init__(self, st, nc, name, depth=24):
            self.sems = [st.enter_context(nc.semaphore(f"{name}{i}")) for i in range(depth)]
            self.tot = [0] * depth
            self.i = 0
            self.depth = depth

        def event(self, n_dmas):
            j = self.i % self.depth
            self.i += 1
            self.tot[j] += 16 * n_dmas
            return self.sems[j], self.tot[j]

        def totals(self):
            return [(self.sems[j], self.tot[j]) for j in range(self.depth) if self.tot[j]]

    def evpool(self, name, depth=24):
        return B.EvPool(self.st, self.nc, name, depth)

    def sb(self, name, shape, dt, stack=None):
        return (stack or self.st).enter_context(self.nc.sbuf_tensor(name, shape, dt))

    def psum(self, name, shape, dt=F32):
        return self.st.enter_context(self.nc.psum_tensor(name, shape, dt))

    def din(self, name, shape, dt=F32):
        return self.nc.dram_tensor(name, shape, dt, kind="ExternalInput").ap()

    def dscr(self, name, shape, dt):
        return self.nc.dram_tensor(name, shape, dt)

    def dsh(self, name, shape, dt):
        return self.nc.dram_tensor(name, shape, dt, addr_space="Shared")


def bview(ap, dup):
    return ap.unsqueeze(2).to_broadcast([ap.shape[0], ap.shape[1], dup])


def build(stop=None, dbg=(), dev_rec=None, dev_own=None, dev_w=None, races=True, sim_poll=True):
    b = B(stop, races=races)
    nc = b.nc
    st = b.st

    own_x = b.din("own_x", [3 + T_OWN, HID])
    rec_x = b.din("rec_x", [T_REC, HID])
    w_int = b.din("w_int", [PROJ // 128, 128, HID // 128, 128])
    w_outT = b.din("w_outT", [INTER, HID])
    conv_w4 = b.din("conv_w4", [128, 80, KTAP])
    conv_bc = b.din("conv_bc", [128, 80])
    dtb_col = b.din("dtb_col", [128, 1])
    alog_col = b.din("alog_col", [128, 1])
    d_cols = b.din("d_cols", [128, 64])
    normw_cols = b.din("normw_cols", [128, 64])
    rmask_cols = b.din("rmask_cols", [128, NCH_REC])
    use_cols = b.din("use_cols", [128, 3])
    nonce = b.din("nonce", [1, 2], I32)
    trimask = b.din("trimask", [128, 128])
    ones_row = b.din("ones_row", [1, 128])
    ident_bf = nc.dram_tensor("ident_bf", [128, 128], BF16, kind="ExternalInput").ap()
    ident_f = b.din("ident_f", [128, 128])
    gq_pk = b.din("gq_pk", [HID // 2, FFN_I // 2], U8)
    uq_pk = b.din("uq_pk", [HID // 2, FFN_I // 2], U8)
    dq_pk = b.din("dq_pk", [FFN_I // 2, HID // 2], U8)
    g_zsT = b.din("g_zsT", [128, FFN_I // 2])
    u_zsT = b.din("u_zsT", [128, FFN_I // 2])
    d_zsT = b.din("d_zsT", [344, HID // 2])
    ind4_i = b.din("ind4_i", [4, 128])

    out = nc.dram_tensor("out", [T_OWN, HID], F32, kind="ExternalOutput").ap()

    xn_scr = b.dscr("xn_scr", [T_OWN, HID], BF16)
    xnr_scr = b.dscr("xnr_scr", [WTOK, HID], BF16)
    halo_scr = b.dscr("halo_scr", [16, HID], BF16)
    sp_gate = b.dscr("sp_gate", [INTER, T_OWN], BF16)
    sp_hs = b.dscr("sp_hs", [INTER, T_OWN], BF16)
    sp_B = b.dscr("sp_B", [G * NST, T_OWN], BF16)
    sp_C = b.dscr("sp_C", [G * NST, T_OWN], BF16)
    sp_dto = b.dscr("sp_dto", [H, T_OWN], F32)
    sp_hsr = b.dscr("sp_hsr", [INTER, T_REC], BF16)
    sp_Br = b.dscr("sp_Br", [G * NST, T_REC], BF16)
    sp_dtr = b.dscr("sp_dtr", [H, T_REC], F32)
    sp_Z = b.dscr("sp_Z", [NCH_OWN, NST, INTER], BF16)
    sp_h = b.dscr("sp_h", [FFN_I, T_OWN], BF16)
    sp_down = b.dscr("sp_down", [T_OWN, HID], F32)
    sp_norm = b.dscr("sp_norm", [INTER, T_OWN], BF16)
    sp_aflat = b.dscr("sp_aflat", [H, CH], F32)
    sp_wg = b.dscr("sp_wg", [128, 43, 32, 128], BF16)
    sp_wu = b.dscr("sp_wu", [128, 43, 32, 128], BF16)
    sp_wd = b.dscr("sp_wd", [FFN_I, HID // 2], BF16)
    dum_scr = b.dscr("dum_scr", [NST, INTER], F32)
    dum_scr2 = b.dscr("dum_scr2", [H, 1], F32)
    dum_scr3 = b.dscr("dum_scr3", [1, 2], I32)

    sh_wg = b.dsh("sh_wg", [128, 86, 32, 128], BF16)
    sh_wu = b.dsh("sh_wu", [128, 86, 32, 128], BF16)
    sh_wd = b.dsh("sh_wd", [FFN_I, HID], BF16)
    sh_S = b.dsh("sh_S", [3, NST, INTER], F32)
    sh_A = b.dsh("sh_A", [3, H, 1], F32)
    sh_flags = b.dsh("sh_flags", [8, 2], I32)

    pg = b.evpool("pg", 24)
    pa = b.evpool("pa", 24)
    psy = b.evpool("py", 24)
    cpe = b.ctr("cpe")
    cdve = b.ctr("cdve")
    cact = b.ctr("cact")
    cpool = b.ctr("cpool")
    chs = b.ctr("chs")

    cw4_sb = b.sb("cw4_sb", [128, 80, KTAP], F32)
    cb_sb = b.sb("cb_sb", [128, 80], F32)
    dtb_sb = b.sb("dtb_sb", [128, 1], F32)
    nega_sb = b.sb("nega_sb", [128, 1], F32)
    dcols_sb = b.sb("dcols_sb", [128, 64], F32)
    nw_sb = b.sb("nw_sb", [128, 64], F32)
    rmask_sb = b.sb("rmask_sb", [128, NCH_REC], F32)
    use_sb = b.sb("use_sb", [128, 3], F32)
    non_sb = b.sb("non_sb", [1, 2], I32)
    tri_sb = b.sb("tri_sb", [128, 128], F32)
    ones_sb = b.sb("ones_sb", [1, 128], F32)
    id_sb = b.sb("id_sb", [128, 128], BF16)
    idf_sb = b.sb("idf_sb", [128, 128], F32)
    zeros_sb = b.sb("zeros_sb", [128, 128], F32)
    eps_sb = b.sb("eps_sb", [128, 1], F32)
    flag_sb = b.sb("flag_sb", [1, 2], I32)
    conv_tails = b.sb("conv_tails", [128, 80, 3], BF16)

    ps = [b.psum(f"ps{i}", [128, 512]) for i in range(6)]
    ps_e = b.psum("ps_e", [128, 512])
    ps_t = b.psum("ps_t", [128, 1024], BF16)

    blk = st.enter_context(nc.Block())

    ev_c = pg.event(13)
    @blk.gpsimd
    def _(g):
        for dst, src in [(cw4_sb, conv_w4), (cb_sb, conv_bc), (dtb_sb, dtb_col),
                         (nega_sb, alog_col), (dcols_sb, d_cols), (nw_sb, normw_cols),
                         (rmask_sb, rmask_cols), (use_sb, use_cols), (non_sb, nonce),
                         (tri_sb, trimask), (ones_sb, ones_row), (id_sb, ident_bf),
                         (idf_sb, ident_f)]:
            g.dma_start(out=dst[:], in_=src).then_inc(ev_c[0], 16)

    @blk.scalar
    def _(s):
        s.wait_ge(*ev_c)
        s.activation(nega_sb[:], nega_sb[:], AF.Exp).then_inc(cact.h, 1)
    cact.v += 1

    @blk.vector
    def _(v, wait=cact.v):
        v.wait_ge(cact.h, wait)
        v.tensor_scalar(nega_sb[:], nega_sb[:], -1.0, None, ALU.mult)
        v.memset(zeros_sb[:], 0.0)
        v.memset(eps_sb[:], EPS)
        v.memset(conv_tails[:], 0.0).then_inc(cdve.h, 1)
    cdve.v += 1
    consts_done = cdve.v

    bst = ExitStack()
    x_tile = b.sb("x_tile", [128, HID], F32, bst)
    sq_scr = b.sb("sq_scr", [128, HID], F32, bst)
    xn_out = b.sb("xn_out", [128, 2, HID], BF16, bst)
    ln_a = b.sb("ln_a", [128, 1], F32, bst)
    ln_b2 = b.sb("ln_b2", [128, 1], F32, bst)
    ln_m = b.sb("ln_m", [128, 1], F32, bst)
    ln_r = b.sb("ln_r", [128, 1], F32, bst)
    ln_mr = b.sb("ln_mr", [128, 1], F32, bst)
    xnT = b.sb("xnT", [128, KC, WTOK], BF16, bst)
    xnT_halo = b.sb("xnT_halo", [128, KC, 16], BF16, bst)
    wt_sb = [b.sb(f"wt{i}", [128, KC, 128], BF16, bst) for i in range(2)]
    conv_in = b.sb("conv_in", [128, 3 + WTOK], BF16, bst)
    cv = b.sb("cv", [128, WTOK], BF16, bst)
    hbc_s = b.sb("hbc_s", [128, 2, WTOK], BF16, bst)
    gev = b.sb("gev", [128, 2, WTOK], BF16, bst)
    dt_st = b.sb("dt_st", [128, WTOK], F32, bst)
    dt_sp = b.sb("dt_sp", [128, WTOK], F32, bst)
    sp_t1 = b.sb("sp_t1", [128, WTOK], F32, bst)
    halo_tm = b.sb("halo_tm", [3, HID], BF16, bst)
    halo_st = b.sb("halo_st", [3, 128], BF16, bst)

    NOEV = (None, 0)
    m = {
        "x_free": (cdve, 0),
        "x_ld": NOEV,
        "xn_free": [NOEV, NOEV],
        "wt_ld": [NOEV, NOEV],
        "wt_free": [0, 0],
        "bank_free": {0: 0, 2: 0},
        "pse_free": 0,
        "hbc_free": [NOEV, NOEV],
        "gev_free": [NOEV, NOEV],
        "conv_free": 0,
        "cv_free": 0,
        "dt_free": NOEV,
        "xnT_ok": NOEV,
        "xnT_free": 0,
        "scr_w": (),
        "tails_w": 0,
    }

    def wge(e, ev):
        if ev[0] is not None:
            e.wait_ge(ev[0], ev[1])

    def emit_ln(src, row0, nrows, scr, scr_row0, halo=False):
        ntile = (nrows + 127) // 128
        scr_evs = []
        for it in range(ntile):
            r = row0 + it * 128
            pr = min(128, nrows - it * 128)
            ob = it % 2
            ld = pa.event(1)
            @blk.scalar
            def _(s, r=r, pr=pr, xf=m["x_free"], ld=ld):
                s.wait_ge(xf[0].h, xf[1]) if xf[1] else None
                s.dma_start(out=x_tile[0:pr, :], in_=src[r:r + pr, :]).then_inc(ld[0], 16)
            m["x_ld"] = ld
            @blk.scalar
            def _(s, pr=pr, ld=ld, xnw=m["xn_free"][ob]):
                s.wait_ge(*ld)
                wge(s, xnw)
                s.activation(sq_scr[0:pr, :], x_tile[0:pr, :], AF.Square,
                             accum_out=ln_b2[0:pr, :]).then_inc(cact.h, 1)
            cact.v += 1
            @blk.vector
            def _(v, pr=pr, wait=cact.v, v0=cdve.v):
                v.wait_ge(cact.h, wait)
                k = v0
                for i, op in enumerate([
                    lambda: v.tensor_scalar(x_tile[0:pr, :], x_tile[0:pr, :], 1.0, 0.0,
                                            ALU.mult, ALU.add, accum_out=ln_a[0:pr, :]),
                    lambda: v.tensor_scalar(ln_m[0:pr, :], ln_a[0:pr, :], 1.0 / HID, None, ALU.mult),
                    lambda: v.tensor_tensor(ln_r[0:pr, :], ln_m[0:pr, :], ln_m[0:pr, :], ALU.mult),
                    lambda: v.tensor_scalar(ln_b2[0:pr, :], ln_b2[0:pr, :], 1.0 / HID, None, ALU.mult),
                    lambda: v.tensor_tensor(ln_r[0:pr, :], ln_b2[0:pr, :], ln_r[0:pr, :], ALU.subtract),
                    lambda: v.tensor_scalar(ln_r[0:pr, :], ln_r[0:pr, :], 0.0, None, ALU.max),
                ]):
                    op().then_inc(cdve.h, 1)
                    k += 1
                    v.wait_ge(cdve.h, k)
            cdve.v += 6
            @blk.scalar
            def _(s, pr=pr, wait=cdve.v, a0=cact.v):
                s.wait_ge(cdve.h, wait)
                s.activation(ln_r[0:pr, :], ln_r[0:pr, :], AF.Ln,
                             bias=eps_sb[0:pr, :]).then_inc(cact.h, 1)
                s.wait_ge(cact.h, a0 + 1)
                s.activation(ln_r[0:pr, :], ln_r[0:pr, :], AF.Exp, scale=-0.5).then_inc(cact.h, 1)
            cact.v += 2
            @blk.vector
            def _(v, pr=pr, ob=ob, wait=cact.v, halo=halo, v0=cdve.v):
                v.wait_ge(cact.h, wait)
                v.tensor_tensor(ln_mr[0:pr, :], ln_m[0:pr, :], ln_r[0:pr, :],
                                ALU.mult).then_inc(cdve.h, 1)
                v.wait_ge(cdve.h, v0 + 1)
                v.tensor_scalar(ln_mr[0:pr, :], ln_mr[0:pr, :], -1.0, None,
                                ALU.mult).then_inc(cdve.h, 1)
                v.wait_ge(cdve.h, v0 + 2)
                o = halo_tm[0:pr, :] if halo else xn_out[0:pr, ob, :]
                v.tensor_scalar(o, x_tile[0:pr, :], ln_r[0:pr, :], ln_mr[0:pr, :],
                                ALU.mult, ALU.add).then_inc(cdve.h, 1)
            cdve.v += 3
            m["x_free"] = (cdve, cdve.v)
            scr_ev = pa.event(1)
            scr_evs.append(scr_ev)
            @blk.scalar
            def _(s, it=it, pr=pr, ob=ob, wait=cdve.v, halo=halo, scr=scr,
                  scr_row0=scr_row0, scr_ev=scr_ev):
                s.wait_ge(cdve.h, wait)
                if halo:
                    s.dma_start(out=scr.ap()[13:16, :], in_=halo_tm[0:3, :]).then_inc(scr_ev[0], 16)
                else:
                    s.dma_start(out=scr.ap()[scr_row0 + it * 128:scr_row0 + it * 128 + pr, :],
                                in_=xn_out[0:pr, ob, :]).then_inc(scr_ev[0], 16)
            m["xn_free"][ob] = scr_ev
        m["scr_w"] = tuple(scr_evs)

    def emit_xnT(scr, row0, ntok, dst, ncol=KC):
        tev = psy.event(ncol)
        for kc in range(ncol):
            @blk.sync
            def _(s, kc=kc, sws=m["scr_w"], xf=m["xnT_free"], tev=tev,
                  row0=row0, ntok=ntok, scr=scr):
                if kc == 0:
                    for sw in sws:
                        s.wait_ge(*sw)
                    if xf:
                        s.wait_ge(cpe.h, xf)
                s.dma_start_transpose(dst[:, kc, 0:ntok],
                                      scr.ap()[row0:row0 + ntok, kc * 128:(kc + 1) * 128]
                                      ).then_inc(tev[0], 16)
        m["xnT_ok"] = tev

    def conv_emit(bb, wofs, ntok, is_rec, evac_wait):
        @blk.vector
        def _(v, bb=bb, ntok=ntok, wait=max(evac_wait, m["cv_free"]), v0=cdve.v):
            v.wait_ge(cact.h, wait)
            kk = v0
            v.tensor_copy(conv_in[:, 0:3], conv_tails[:, bb, :]).then_inc(cdve.h, 1)
            kk += 1
            v.wait_ge(cdve.h, kk)
            v.tensor_scalar(cv[:, 0:ntok], conv_in[:, 0:ntok], cw4_sb[:, bb, 0:1],
                            None, ALU.mult).then_inc(cdve.h, 1)
            kk += 1
            v.wait_ge(cdve.h, kk)
            for k in range(1, KTAP):
                v.scalar_tensor_tensor(cv[:, 0:ntok], conv_in[:, k:k + ntok],
                                       cw4_sb[:, bb, k:k + 1], cv[:, 0:ntok],
                                       ALU.mult, ALU.add).then_inc(cdve.h, 1)
                kk += 1
                v.wait_ge(cdve.h, kk)
            v.tensor_copy(conv_tails[:, bb, :], conv_in[:, ntok:ntok + 3]).then_inc(cdve.h, 1)
        cdve.v += 6
        m["conv_free"] = cdve.v
        hb = bb % 2
        @blk.scalar
        def _(s, bb=bb, ntok=ntok, hb=hb, wait=cdve.v, hw=m["hbc_free"][hb]):
            s.wait_ge(cdve.h, wait)
            wge(s, hw)
            s.activation(hbc_s[:, hb, 0:ntok], cv[:, 0:ntok], AF.Silu,
                         bias=cb_sb[:, bb:bb + 1]).then_inc(cact.h, 1)
        cact.v += 1
        m["cv_free"] = cact.v
        if is_rec:
            tgt, row = (sp_hsr, bb * 128) if bb < 64 else (sp_Br, (bb - 64) * 128)
        else:
            tgt, row = (sp_hs, bb * 128) if bb < 64 else \
                       ((sp_B, (bb - 64) * 128) if bb < 72 else (sp_C, (bb - 72) * 128))
        sev = pa.event(1)
        @blk.scalar
        def _(s, tgt=tgt, row=row, wofs=wofs, ntok=ntok, hb=hb, wait=cact.v, sev=sev):
            s.wait_ge(cact.h, wait)
            s.dma_start(out=tgt.ap()[row:row + 128, wofs:wofs + ntok],
                        in_=hbc_s[:, hb, 0:ntok]).then_inc(sev[0], 16)
        m["hbc_free"][hb] = sev

    def emit_window(is_rec, wofs, ntok, with_halo, halo_ev=None):
        if is_rec:
            jlist = [J_DT] + list(range(64, 136))
        else:
            jlist = [J_DT] + list(range(64, 144)) + list(range(0, 64))
        nt5 = (ntok + 511) // 512
        for ji, jb in enumerate(jlist):
            buf = ji % 2
            wev = pg.event(1)
            @blk.gpsimd
            def _(g, jb=jb, buf=buf, wfree=m["wt_free"][buf], wev=wev):
                if wfree:
                    g.wait_ge(cpe.h, wfree)
                g.dma_start(out=wt_sb[buf][:],
                            in_=w_int[jb]).then_inc(wev[0], 16)
            m["wt_ld"][buf] = wev
            bank0 = 0 if ji % 2 == 0 else 2
            is_hbc = 64 <= jb < 144
            do_halo = with_halo and is_hbc
            @blk.tensor
            def _(t, jb=jb, buf=buf, bank0=bank0, nt5=nt5, ntok=ntok,
                  wev=wev, xev=m["xnT_ok"], bwait=m["bank_free"][bank0],
                  pwait=m["pse_free"] if do_halo else 0, do_halo=do_halo,
                  halo_ev=halo_ev if do_halo else None):
                t.wait_ge(*wev)
                t.wait_ge(*xev)
                if bwait:
                    t.wait_ge(cact.h, bwait)
                for t5 in range(nt5):
                    n = min(512, ntok - t5 * 512)
                    for kc in range(KC):
                        ins = t.matmul(ps[bank0 + t5][:, 0:n], wt_sb[buf][:, kc, :],
                                       xnT[:, kc, t5 * 512:t5 * 512 + n],
                                       start=(kc == 0), stop=(kc == KC - 1))
                        if t5 == nt5 - 1 and kc == KC - 1:
                            ins.then_inc(cpe.h, 1)
                if do_halo:
                    if pwait:
                        t.wait_ge(cact.h, pwait)
                    if halo_ev is not None:
                        t.wait_ge(*halo_ev)
                    for kc in range(KC):
                        ins = t.matmul(ps_e[0:3, 0:128], xnT_halo[:, kc, 13:16],
                                       wt_sb[buf][:, kc, :],
                                       start=(kc == 0), stop=(kc == KC - 1))
                        if kc == KC - 1:
                            ins.then_inc(cpe.h, 1)
            cpe.v += 1 + (1 if do_halo else 0)
            m["wt_free"][buf] = cpe.v
            pe_main = cpe.v - (1 if do_halo else 0)
            pe_halo = cpe.v if do_halo else 0

            if do_halo:
                @blk.scalar
                def _(s, wait=pe_halo):
                    s.wait_ge(cpe.h, wait)
                    s.activation(halo_st[0:3, :], ps_e[0:3, 0:128], AF.Copy).then_inc(cact.h, 1)
                cact.v += 1
                m["pse_free"] = cact.v
                @blk.tensor
                def _(t, wait=cact.v, tw=m["tails_w"]):
                    t.wait_ge(cact.h, wait)
                    if tw:
                        t.wait_ge(cdve.h, tw)
                    t.transpose(ps_t[:, 0:3], halo_st[0:3, :], id_sb[0:3, 0:3]).then_inc(cpe.h, 1)
                cpe.v += 1
                @blk.vector
                def _(v, jb=jb, wait=cpe.v):
                    v.wait_ge(cpe.h, wait)
                    v.tensor_copy(conv_tails[:, jb - 64, :], ps_t[:, 0:3]).then_inc(cdve.h, 1)
                cdve.v += 1
                m["tails_w"] = cdve.v

            if jb == J_DT:
                @blk.scalar
                def _(s, nt5=nt5, ntok=ntok, bank0=bank0, wait=pe_main, dtw=m["dt_free"]):
                    s.wait_ge(cpe.h, wait)
                    wge(s, dtw)
                    for t5 in range(nt5):
                        n = min(512, ntok - t5 * 512)
                        s.activation(dt_st[:, t5 * 512:t5 * 512 + n], ps[bank0 + t5][:, 0:n],
                                     AF.Copy).then_inc(cact.h, 1)
                cact.v += nt5
                m["bank_free"][bank0] = cact.v
                @blk.vector
                def _(v, ntok=ntok, wait=cact.v):
                    v.wait_ge(cact.h, wait)
                    v.tensor_scalar(dt_st[:, 0:ntok], dt_st[:, 0:ntok], dtb_sb[:, 0:1],
                                    None, ALU.add).then_inc(cdve.h, 1)
                cdve.v += 1
                @blk.scalar
                def _(s, ntok=ntok, wait=cdve.v, a0=cact.v):
                    s.wait_ge(cdve.h, wait)
                    s.activation(sp_t1[:, 0:ntok], dt_st[:, 0:ntok], AF.Abs).then_inc(cact.h, 1)
                    s.wait_ge(cact.h, a0 + 1)
                    s.activation(sp_t1[:, 0:ntok], sp_t1[:, 0:ntok], AF.Exp,
                                 scale=-1.0).then_inc(cact.h, 1)
                    s.wait_ge(cact.h, a0 + 2)
                    s.activation(sp_t1[:, 0:ntok], sp_t1[:, 0:ntok], AF.Ln,
                                 bias=1.0).then_inc(cact.h, 1)
                cact.v += 3
                @blk.vector
                def _(v, ntok=ntok, wait=cact.v, v0=cdve.v):
                    v.wait_ge(cact.h, wait)
                    v.tensor_scalar(dt_sp[:, 0:ntok], dt_st[:, 0:ntok], 0.0, None,
                                    ALU.max).then_inc(cdve.h, 1)
                    v.wait_ge(cdve.h, v0 + 1)
                    v.tensor_tensor(dt_sp[:, 0:ntok], dt_sp[:, 0:ntok], sp_t1[:, 0:ntok],
                                    ALU.add).then_inc(cdve.h, 1)
                cdve.v += 2
                tgt = sp_dtr if is_rec else sp_dto
                dev_ = pa.event(1)
                @blk.scalar
                def _(s, tgt=tgt, wofs=wofs, ntok=ntok, wait=cdve.v, dev_=dev_):
                    s.wait_ge(cdve.h, wait)
                    s.dma_start(out=tgt.ap()[:, wofs:wofs + ntok],
                                in_=dt_sp[:, 0:ntok]).then_inc(dev_[0], 16)
                m["dt_free"] = dev_
            elif is_hbc:
                bb = jb - 64
                @blk.scalar
                def _(s, nt5=nt5, ntok=ntok, bank0=bank0, wait=pe_main,
                      cw=m["conv_free"], tw=m["tails_w"]):
                    s.wait_ge(cpe.h, wait)
                    w = max(cw, tw)
                    if w:
                        s.wait_ge(cdve.h, w)
                    for t5 in range(nt5):
                        n = min(512, ntok - t5 * 512)
                        s.activation(conv_in[:, 3 + t5 * 512:3 + t5 * 512 + n],
                                     ps[bank0 + t5][:, 0:n], AF.Copy).then_inc(cact.h, 1)
                cact.v += nt5
                m["bank_free"][bank0] = cact.v
                conv_emit(bb, wofs, ntok, is_rec, cact.v)
            else:
                hb = ji % 2
                @blk.scalar
                def _(s, nt5=nt5, ntok=ntok, bank0=bank0, hb=hb, wait=pe_main,
                      gw=m["gev_free"][hb]):
                    s.wait_ge(cpe.h, wait)
                    wge(s, gw)
                    for t5 in range(nt5):
                        n = min(512, ntok - t5 * 512)
                        s.activation(gev[:, hb, t5 * 512:t5 * 512 + n],
                                     ps[bank0 + t5][:, 0:n], AF.Copy).then_inc(cact.h, 1)
                cact.v += nt5
                m["bank_free"][bank0] = cact.v
                gv = pa.event(1)
                @blk.scalar
                def _(s, jb=jb, hb=hb, wofs=wofs, ntok=ntok, wait=cact.v, gv=gv):
                    s.wait_ge(cact.h, wait)
                    s.dma_start(out=sp_gate.ap()[jb * 128:(jb + 1) * 128, wofs:wofs + ntok],
                                in_=gev[:, hb, 0:ntok]).then_inc(gv[0], 16)
                m["gev_free"][hb] = gv
        m["xnT_free"] = cpe.v

    rec_widths = ([1024] * 6 + [128]) if dev_rec is None else dev_rec
    wofs = 0
    for wi, wn in enumerate(rec_widths):
        emit_ln(rec_x, wofs, wn, xnr_scr, 0)
        emit_xnT(xnr_scr, 0, wn, xnT)
        emit_window(True, wofs, wn, False)
        wofs += wn

    @blk.vector
    def _(v):
        v.memset(xn_out[0:13, 0, :], 0.0).then_inc(cdve.h, 1)
    cdve.v += 1
    zev = pa.event(1)
    @blk.scalar
    def _(s, wait=cdve.v, zev=zev):
        s.wait_ge(cdve.h, wait)
        s.dma_start(out=halo_scr.ap()[0:13, :], in_=xn_out[0:13, 0, :]).then_inc(zev[0], 16)
    m["xn_free"][0] = zev
    emit_ln(own_x, 0, 3, halo_scr, 0, halo=True)
    emit_xnT(halo_scr, 0, 16, xnT_halo)
    halo_ev = m["xnT_ok"]
    for wi in range(2 if dev_own is None else dev_own):
        emit_ln(own_x, 3 + wi * WTOK, WTOK, xn_scr, wi * WTOK)
        emit_xnT(xn_scr, wi * WTOK, WTOK, xnT)
        emit_window(False, wi * WTOK, WTOK, wi == 0, halo_ev=halo_ev)

    def join_all():
        @blk.scalar
        def _(s, evs=tuple(pa.totals())):
            for ev in evs:
                s.wait_ge(*ev)
            s.nop().then_inc(chs.h, 1)
        chs.v += 1
        @blk.sync
        def _(s, evs=tuple(psy.totals())):
            for ev in evs:
                s.wait_ge(*ev)
            s.nop().then_inc(chs.h, 1)
        chs.v += 1
        @blk.gpsimd
        def _(g, evs=tuple(pg.totals()), w3=cdve.v, w4=cact.v, w5=cpe.v, wc=chs.v):
            for ev in evs:
                g.wait_ge(*ev)
            g.wait_ge(cdve.h, w3)
            g.wait_ge(cact.h, w4)
            g.wait_ge(cpe.h, w5)
            g.wait_ge(chs.h, wc)
            g.nop().then_inc(chs.h, 1)
        chs.v += 1
        return chs.v

    if stop == "bwin":
        dumps = [("xn_scr", xn_scr, [T_OWN, HID], BF16),
                 ("sp_hs", sp_hs, [INTER, T_OWN], BF16),
                 ("sp_B", sp_B, [G * NST, T_OWN], BF16),
                 ("sp_C", sp_C, [G * NST, T_OWN], BF16),
                 ("sp_gate", sp_gate, [INTER, T_OWN], BF16),
                 ("sp_dto", sp_dto, [H, T_OWN], F32),
                 ("sp_hsr", sp_hsr, [INTER, T_REC], BF16),
                 ("sp_Br", sp_Br, [G * NST, T_REC], BF16),
                 ("sp_dtr", sp_dtr, [H, T_REC], F32)]
        douts = {nm: nc.dram_tensor("o_" + nm, shape, dt, kind="ExternalOutput").ap()
                 for nm, _, shape, dt in dumps}
        join_all()
        dev2 = pg.event(len(dumps))
        @blk.gpsimd
        def _(g, dev2=dev2):
            for nm, srct, shape, dt in dumps:
                g.dma_start(out=douts[nm], in_=srct.ap()).then_inc(dev2[0], 16)
            g.wait_ge(*dev2)
        bst.close()
        st.close()
        return b

    jv_b = join_all()
    bst.close()

    sst = ExitStack()
    s_true = b.sb("s_true", [128, INTER], F32, sst)
    acum = b.sb("acum", [128, 1], F32, sst)
    xt = b.sb("xt", [128, INTER], BF16, sst)
    dtl = b.sb("dtl", [128, CH], F32, sst)
    adt = b.sb("adt", [128, CH], F32, sst)
    acs = b.sb("acs", [128, CH], F32, sst)
    dtt_bf = b.sb("dtt_bf", [128, CH], BF16, sst)
    eacol = b.sb("eacol", [128, 1], F32, sst)
    earow = b.sb("earow", [1, 128], F32, sst)
    eah = b.sb("eah", [128, 128], F32, sst)
    tbuf = b.sb("tbuf", [128, 512], F32, sst)
    d0st = ExitStack()
    bt = b.sb("bt", [128, G * NST], BF16, d0st)
    zbuf = b.sb("zbuf", [128, INTER], BF16, d0st)
    dec_sb = b.sb("dec_sb", [128, CH], F32, d0st)
    dect_bf = b.sb("dect_bf", [128, CH], BF16, d0st)

    for eng in ("vector", "scalar", "tensor", "sync", "gpsimd"):
        @getattr(blk, eng)
        def _(e, w=jv_b):
            e.wait_ge(chs.h, w)

    @blk.vector
    def _(v):
        v.memset(s_true[:], 0.0)
        v.memset(acum[:], 0.0).then_inc(cdve.h, 1)
    cdve.v += 1

    def emit_d0(is_rec, nch, src_hs, src_B, src_dt, spill_z):
        for c in range(nch):
            dtev = pa.event(1)
            @blk.scalar
            def _(s, c=c, dtev=dtev, src_dt=src_dt, wpe=cpe.v, wdve=cdve.v):
                s.wait_ge(cpe.h, wpe)
                s.wait_ge(cdve.h, wdve)
                s.dma_start(out=dtl[:], in_=src_dt.ap()[:, c * CH:(c + 1) * CH]
                            ).then_inc(dtev[0], 16)
            @blk.vector
            def _(v, c=c, dtev=dtev, is_rec=is_rec, v0=cdve.v):
                v.wait_ge(*dtev)
                k = v0
                if is_rec:
                    v.tensor_scalar(dtl[:], dtl[:], rmask_sb[:, c:c + 1], None,
                                    ALU.mult).then_inc(cdve.h, 1)
                    k += 1
                    v.wait_ge(cdve.h, k)
                v.tensor_scalar(adt[:], dtl[:], nega_sb[:, 0:1], None,
                                ALU.mult).then_inc(cdve.h, 1)
                k += 1
                v.wait_ge(cdve.h, k)
                v.tensor_tensor_scan(acs[:], adt[:], zeros_sb[:, 0:128], 0.0,
                                     ALU.add, ALU.add).then_inc(cdve.h, 1)
                k += 1
                v.wait_ge(cdve.h, k)
                v.tensor_tensor(acum[:], acum[:], acs[:, CH - 1:CH],
                                ALU.add).then_inc(cdve.h, 1)
            cdve.v += 4 if is_rec else 3
            dve_acs = cdve.v
            @blk.scalar
            def _(s, wait=dve_acs, a0=cact.v):
                s.wait_ge(cdve.h, wait)
                s.activation(dec_sb[:], acs[:], AF.Exp, scale=-1.0,
                             bias=acs[:, CH - 1:CH]).then_inc(cact.h, 1)
                s.activation(eacol[:], acs[:, CH - 1:CH], AF.Exp).then_inc(cact.h, 1)
            cact.v += 2
            act_dec = cact.v
            @blk.tensor
            def _(t, wait=act_dec, dve_w=dve_acs):
                t.wait_ge(cact.h, wait)
                t.transpose(ps_e[:, 0:128], dtl[:], idf_sb[:]).then_inc(cpe.h, 1)
            cpe.v += 1
            @blk.vector
            def _(v, wait=cpe.v):
                v.wait_ge(cpe.h, wait)
                v.tensor_copy(dtt_bf[:], ps_e[:, 0:128]).then_inc(cdve.h, 1)
            cdve.v += 1
            @blk.tensor
            def _(t, wait=cdve.v):
                t.wait_ge(cdve.h, wait)
                t.transpose(ps_e[:, 0:128], dec_sb[:], idf_sb[:]).then_inc(cpe.h, 1)
            cpe.v += 1
            @blk.vector
            def _(v, wait=cpe.v):
                v.wait_ge(cpe.h, wait)
                v.tensor_copy(dect_bf[:], ps_e[:, 0:128]).then_inc(cdve.h, 1)
            cdve.v += 1
            @blk.tensor
            def _(t, wait=cdve.v):
                t.wait_ge(cdve.h, wait)
                t.transpose(ps_e[0:1, 0:128], eacol[:], idf_sb[:]).then_inc(cpe.h, 1)
            cpe.v += 1
            @blk.vector
            def _(v, wait=cpe.v):
                v.wait_ge(cpe.h, wait)
                v.tensor_copy(earow[:], ps_e[0:1, 0:128]).then_inc(cdve.h, 1)
            cdve.v += 1
            @blk.tensor
            def _(t, wait=cdve.v):
                t.wait_ge(cdve.h, wait)
                t.matmul(ps_e[:, 0:128], ones_sb[:], earow[:], start=True,
                         stop=True).then_inc(cpe.h, 1)
            cpe.v += 1
            @blk.vector
            def _(v, wait=cpe.v):
                v.wait_ge(cpe.h, wait)
                v.tensor_copy(eah[:], ps_e[:, 0:128]).then_inc(cdve.h, 1)
            cdve.v += 1
            eah_ok = cdve.v
            xev = psy.event(1)
            @blk.sync
            def _(s, c=c, xev=xev, src_hs=src_hs, gate=cdve.v):
                s.wait_ge(cdve.h, gate)
                s.dma_start_transpose(xt[:], src_hs.ap()[:, c * CH:(c + 1) * CH]
                                      ).then_inc(xev[0], 16)
            bev = psy.event(1)
            @blk.sync
            def _(s, c=c, bev=bev, src_B=src_B):
                s.dma_start_transpose(bt[:], src_B.ap()[:, c * CH:(c + 1) * CH]
                                      ).then_inc(bev[0], 16)
            @blk.vector
            def _(v, xev=xev, v0=cdve.v):
                v.wait_ge(*xev)
                v.tensor_tensor(
                    xt[:].rearrange("p (h q) -> p h q", h=H),
                    xt[:].rearrange("p (h q) -> p h q", h=H),
                    dtt_bf[:].unsqueeze(2).to_broadcast([128, H, PD]),
                    ALU.mult).then_inc(cdve.h, 1)
                v.wait_ge(cdve.h, v0 + 1)
                v.tensor_tensor(
                    xt[:].rearrange("p (h q) -> p h q", h=H),
                    xt[:].rearrange("p (h q) -> p h q", h=H),
                    dect_bf[:].unsqueeze(2).to_broadcast([128, H, PD]),
                    ALU.mult).then_inc(cdve.h, 1)
            cdve.v += 2
            x2_ok = cdve.v
            for j in range(16):
                bank = ps[j % 2]
                @blk.tensor
                def _(t, j=j, bank=bank, bev=bev, xw=x2_ok,
                      bfree=m["bank_free"].get(("z", j % 2), 0)):
                    if j < 2:
                        t.wait_ge(*bev)
                        t.wait_ge(cdve.h, xw)
                    if bfree:
                        t.wait_ge(cdve.h, bfree)
                    for q in range(8):
                        h = 8 * j + q
                        ins = t.matmul(bank[:, q * PD:(q + 1) * PD],
                                       bt[:, q * NST:(q + 1) * NST],
                                       xt[:, h * PD:(h + 1) * PD],
                                       start=True, stop=True)
                        if q == 7:
                            ins.then_inc(cpe.h, 1)
                cpe.v += 1
                @blk.vector
                def _(v, j=j, bank=bank, wait=cpe.v, ew=eah_ok, v0=cdve.v,
                      spill_z=spill_z):
                    v.wait_ge(cpe.h, wait)
                    v.wait_ge(cdve.h, max(ew, v0))
                    sl = slice(j * 512, (j + 1) * 512)
                    v.tensor_tensor(
                        tbuf[:].rearrange("p (h q) -> p h q", h=8),
                        s_true[:, sl].rearrange("p (h q) -> p h q", h=8),
                        eah[:, j * 8:(j + 1) * 8].unsqueeze(2).to_broadcast([128, 8, PD]),
                        ALU.mult).then_inc(cdve.h, 1)
                    v.wait_ge(cdve.h, v0 + 1)
                    if spill_z:
                        v.tensor_copy(zbuf[:, sl], bank[:]).then_inc(cdve.h, 1)
                        v.wait_ge(cdve.h, v0 + 2)
                    v.tensor_tensor(s_true[:, sl], tbuf[:], bank[:],
                                    ALU.add).then_inc(cdve.h, 1)
                cdve.v += 3 if spill_z else 2
                m["bank_free"][("z", j % 2)] = cdve.v
            if spill_z:
                zev = pa.event(1)
                @blk.scalar
                def _(s, c=c, wait=cdve.v, zev=zev):
                    s.wait_ge(cdve.h, wait)
                    s.dma_start(out=sp_Z.ap()[c], in_=zbuf[:]).then_inc(zev[0], 16)
                m["z_spill"] = zev
                @blk.vector
                def _(v, zev=zev):
                    v.wait_ge(*zev)

    def emit_publish(slot_S, flag_row, slots=(0, 1)):
        dev_ = pg.event(2)
        fev = pg.event(1)
        @blk.gpsimd
        def _(g, w=cdve.v, dev_=dev_, fev=fev):
            g.wait_ge(cdve.h, w)
            rank = nc.partition_id()
            for r in g.Switch(rank, 8):
                s = r % 2
                if s in slots:
                    tS = slot_S(s)
                    g.dma_start(out=sh_S.ap()[tS], in_=s_true[:]).then_inc(dev_[0], 16)
                    g.dma_start(out=sh_A.ap()[tS], in_=acum[:]).then_inc(dev_[0], 16)
                    g.wait_ge(*dev_)
                    g.dma_start(out=sh_flags.ap()[flag_row(s):flag_row(s) + 1, :],
                                in_=non_sb[:]).then_inc(fev[0], 16)
                else:
                    g.dma_start(out=dum_scr.ap(), in_=s_true[:]).then_inc(dev_[0], 16)
                    g.dma_start(out=dum_scr2.ap(), in_=acum[:]).then_inc(dev_[0], 16)
                    g.wait_ge(*dev_)
                    g.dma_start(out=dum_scr3.ap(), in_=non_sb[:]).then_inc(fev[0], 16)
        return dev_

    def emit_poll(rows):
        if not sim_poll:
            @blk.scalar
            def _(s):
                s.nop().then_inc(chs.h, 1)
            chs.v += 1
            return chs.v
        pn = m["polln"] = m.get("polln", 0) + 1
        pdl = b.ctr(f"pdl{pn}")
        @blk.scalar
        def _(s, rows=tuple(rows), pn=pn, pdl=pdl):
            with (
                s.register(f"cont{pn}") as cont,
                s.register(f"f0_{pn}") as f0,
                s.register(f"f1_{pn}") as f1,
                s.register(f"n0_{pn}") as n0,
                s.register(f"n1_{pn}") as n1,
                s.register(f"eq0_{pn}") as eq0,
                s.register(f"both{pn}") as both,
                s.register(f"tgt{pn}") as tgt,
            ):
                s.reg_load(n0, non_sb[0:1, 0:1])
                s.reg_load(n1, non_sb[0:1, 1:2])
                s.reg_mov(tgt, 0)
                for fr in rows:
                    s.reg_mov(cont, 1)
                    with s.While(cont):
                        s.reg_alu(tgt, tgt, 16, ALU.add)
                        s.dma_start(out=flag_sb[:], in_=sh_flags.ap()[fr:fr + 1, :]
                                    ).then_inc(pdl.h, 16)
                        s.wait_ge(pdl.h, tgt)
                        s.reg_load(f0, flag_sb[0:1, 0:1])
                        s.reg_load(f1, flag_sb[0:1, 1:2])
                        s.reg_alu(eq0, f0, n0, ALU.is_equal)
                        s.reg_alu(both, f1, n1, ALU.is_equal)
                        s.reg_alu(both, both, eq0, ALU.bitwise_and)
                        s.reg_alu(cont, both, 1, ALU.bitwise_xor)
            s.nop().then_inc(chs.h, 1)
        chs.v += 1
        return chs.v

    rec_nch = sum(rec_widths) // CH
    own_nch = ((2 if dev_own is None else dev_own) * WTOK) // CH
    if rec_nch:
        emit_d0(True, rec_nch, sp_hsr, sp_Br, sp_dtr, False)
    pub_ev = emit_publish(lambda s: s, lambda s: 2 + s)
    @blk.vector
    def _(v, w=cdve.v, pub_ev=pub_ev):
        v.wait_ge(cdve.h, w)
        v.wait_ge(*pub_ev)
        v.memset(s_true[:], 0.0)
        v.memset(acum[:], 0.0).then_inc(cdve.h, 1)
    cdve.v += 1
    emit_d0(False, own_nch, sp_hs, sp_B, sp_dto, True)
    emit_publish(lambda s: 2, lambda s: 4, slots=(0,))
    d0st.close()

    if stop == "d0":
        join_all()
        sst.close()
        st.close()
        return b

    wst = ExitStack()
    wq_sb = b.sb("wq_sb", [128, 2, 512], U8, wst)
    t8a = b.sb("t8a", [128, 512], U8, wst)
    t8b = b.sb("t8b", [128, 512], U8, wst)
    fhi = b.sb("fhi", [128, 512], F32, wst)
    flo = b.sb("flo", [128, 512], F32, wst)
    dqh = b.sb("dqh", [128, 2, 512], BF16, wst)
    dql = b.sb("dql", [128, 2, 512], BF16, wst)
    zrow_sb = b.sb("zrow_sb", [4, 2, FFN_I // 2], F32, wst)
    srow_sb = b.sb("srow_sb", [4, 2, FFN_I // 2], F32, wst)
    ind4_sb = b.sb("ind4_sb", [4, 128], F32, wst)

    cev_w = pa.event(1)
    @blk.scalar
    def _(s, cev=cev_w):
        s.dma_start(out=ind4_sb[:], in_=ind4_i).then_inc(cev[0], 16)
    @blk.tensor
    def _(e, cev=cev_w):
        e.wait_ge(*cev)

    WCFG = [
        (gq_pk, g_zsT, 64, sp_wg, 16, [512] * 10 + [384]),
        (uq_pk, u_zsT, 64, sp_wu, 16, [512] * 10 + [384]),
        (dq_pk, d_zsT, 172, sp_wd, 43, [512] * 4),
    ]
    if dev_w is not None:
        WCFG = [(a, z, g_, d_, min(n, dev_w), o) for (a, z, g_, d_, n, o) in WCFG]
    w0_dve = cdve.v
    w0_pool = cpool.v
    gn = 0
    for wq_ap, zs_ap, GR, dst_scr, nkb, obs in WCFG:
        ocols = sum(obs)
        for kb in range(nkb):
            zev = pa.event(2)
            kslot = kb % 2
            @blk.scalar
            def _(s, kb=kb, kslot=kslot, zev=zev, zs_ap=zs_ap, GR=GR, oc=ocols,
                  zw=m.get(("zsf", kslot), 0)):
                if zw:
                    s.wait_ge(cpe.h, zw)
                s.dma_start(out=zrow_sb[:, kslot, 0:oc],
                            in_=zs_ap[4 * kb:4 * kb + 4]).then_inc(zev[0], 16)
                s.dma_start(out=srow_sb[:, kslot, 0:oc],
                            in_=zs_ap[GR + 4 * kb:GR + 4 * kb + 4]).then_inc(zev[0], 16)
            off = 0
            for W in obs:
                p2 = gn % 2
                wev = pa.event(1)
                @blk.scalar
                def _(s, kb=kb, p2=p2, off=off, W=W, wev=wev, wq_ap=wq_ap,
                      tw=m.get(("wqf", p2), 0)):
                    if tw:
                        s.wait_ge(cdve.h, tw)
                    s.dma_start(out=wq_sb[:, p2, 0:W],
                                in_=wq_ap[128 * kb:128 * (kb + 1), off:off + W]
                                ).then_inc(wev[0], 16)
                bz = ps[2 + 2 * p2]
                bs_ = ps[3 + 2 * p2]
                @blk.tensor
                def _(t, kslot=kslot, off=off, W=W, zev=zev, bz=bz, bs_=bs_,
                      bw=m.get(("wbank", p2), 0), w0=w0_dve):
                    t.wait_ge(*zev)
                    if bw:
                        t.wait_ge(cdve.h, bw)
                    else:
                        t.wait_ge(cdve.h, w0)
                    t.matmul(bz[:, 0:W], ind4_sb[:], zrow_sb[:, kslot, off:off + W],
                             start=True, stop=True).then_inc(cpe.h, 1)
                    t.matmul(bs_[:, 0:W], ind4_sb[:], srow_sb[:, kslot, off:off + W],
                             start=True, stop=True).then_inc(cpe.h, 1)
                cpe.v += 2
                m[("zsf", kslot)] = cpe.v
                ndma = 2 if dst_scr is not sp_wd else 1
                oevh = pg.event(ndma)
                oevl = pg.event(ndma)
                @blk.vector
                def _(v, p2=p2, W=W, wev=wev, bz=bz, bs_=bs_, pw=cpe.v,
                      owh=m.get(("dqfh", p2), None), owl=m.get(("dqfl", p2), None),
                      v0=cdve.v):
                    v.wait_ge(*wev)
                    v.tensor_scalar(t8a[:, 0:W], wq_sb[:, p2, 0:W], 4, 15,
                                    ALU.logical_shift_right,
                                    ALU.bitwise_and).then_inc(cdve.h, 1)
                    v.tensor_scalar(t8b[:, 0:W], wq_sb[:, p2, 0:W], 15, None,
                                    ALU.bitwise_and).then_inc(cdve.h, 1)
                    v.wait_ge(cdve.h, v0 + 1)
                    v.tensor_copy(fhi[:, 0:W], t8a[:, 0:W]).then_inc(cdve.h, 1)
                    v.wait_ge(cdve.h, v0 + 2)
                    v.tensor_copy(flo[:, 0:W], t8b[:, 0:W]).then_inc(cdve.h, 1)
                    v.wait_ge(cdve.h, v0 + 4)
                    v.wait_ge(cpe.h, pw)
                    v.tensor_tensor(fhi[:, 0:W], fhi[:, 0:W], bz[:, 0:W],
                                    ALU.subtract).then_inc(cdve.h, 1)
                    v.tensor_tensor(flo[:, 0:W], flo[:, 0:W], bz[:, 0:W],
                                    ALU.subtract).then_inc(cdve.h, 1)
                    v.wait_ge(cdve.h, v0 + 5)
                    if owh is not None:
                        v.wait_ge(*owh)
                    v.tensor_tensor(dqh[:, p2, 0:W], fhi[:, 0:W], bs_[:, 0:W],
                                    ALU.mult).then_inc(cdve.h, 1)
                    v.wait_ge(cdve.h, v0 + 6)
                    if owl is not None:
                        v.wait_ge(*owl)
                    v.tensor_tensor(dql[:, p2, 0:W], flo[:, 0:W], bs_[:, 0:W],
                                    ALU.mult).then_inc(cdve.h, 1)
                cdve.v += 8
                m[("wqf", p2)] = cdve.v - 6
                m[("wbank", p2)] = cdve.v
                tiled = dst_scr is not sp_wd
                @blk.gpsimd
                def _(g, p2=p2, W=W, w=cdve.v, oevh=oevh, oevl=oevl,
                      dst_scr=dst_scr, kb=kb, off=off, tiled=tiled):
                    g.wait_ge(cdve.h, w)
                    if tiled:
                        tv = dst_scr.ap().rearrange(
                            "(pp two) fb (kcb kc2) ii -> two kcb kc2 pp fb ii",
                            two=2, kc2=2)
                        fb0, nfb = off // 128, W // 128
                        for a in range(2):
                            g.dma_start(
                                out=tv[0][kb][a][:, fb0:fb0 + nfb, :],
                                in_=dqh[a * 64:(a + 1) * 64, p2, 0:W]
                                ).then_inc(oevh[0], 16)
                            g.dma_start(
                                out=tv[1][kb][a][:, fb0:fb0 + nfb, :],
                                in_=dql[a * 64:(a + 1) * 64, p2, 0:W]
                                ).then_inc(oevl[0], 16)
                    else:
                        dvw = dst_scr.ap().rearrange("(kk two) o -> two kk o", two=2)
                        g.dma_start(out=dvw[0][128 * kb:128 * (kb + 1), off:off + W],
                                    in_=dqh[:, p2, 0:W]).then_inc(oevh[0], 16)
                        g.dma_start(out=dvw[1][128 * kb:128 * (kb + 1), off:off + W],
                                    in_=dql[:, p2, 0:W]).then_inc(oevl[0], 16)
                m[("dqfh", p2)] = oevh
                m[("dqfl", p2)] = oevl
                off += W
                gn += 1

    wtotals = tuple(pg.totals())
    @blk.gpsimd
    def _(g, evs=wtotals):
        for ev in evs:
            g.wait_ge(*ev)
        g.nop().then_inc(chs.h, 1)
    chs.v += 1
    wfence = chs.v
    for eng in ("vector", "scalar", "tensor", "sync"):
        @getattr(blk, eng)
        def _(e, w=wfence):
            e.wait_ge(chs.h, w)
    wc_ev = pg.event(3)
    wf_ev = pg.event(1)
    @blk.gpsimd
    def _(g, wc_ev=wc_ev, wf_ev=wf_ev):
        rank = nc.partition_id()
        hw = FFN_I // 2
        hw2 = HID // 2
        for r in g.Switch(rank, 8):
            s_ = r % 2
            g.dma_start(out=sh_wg.ap()[:, s_ * 43:(s_ + 1) * 43],
                        in_=sp_wg.ap()).then_inc(wc_ev[0], 16)
            g.dma_start(out=sh_wu.ap()[:, s_ * 43:(s_ + 1) * 43],
                        in_=sp_wu.ap()).then_inc(wc_ev[0], 16)
            g.dma_start(out=sh_wd.ap()[:, s_ * hw2:(s_ + 1) * hw2],
                        in_=sp_wd.ap()).then_inc(wc_ev[0], 16)
            g.wait_ge(*wc_ev)
            g.dma_start(out=sh_flags.ap()[s_:s_ + 1, :],
                        in_=non_sb[:]).then_inc(wf_ev[0], 16)
    wst.close()

    if stop == "wphase":
        join_all()
        sst.close()
        st.close()
        return b

    pw = emit_poll([2, 3, 4])
    for eng in ("vector", "scalar", "tensor", "gpsimd"):
        @getattr(blk, eng)
        def _(e, w=pw):
            e.wait_ge(chs.h, w)

    est = ExitStack()
    shs_sl = b.sb("shs_sl", [128, 2, 512], F32, est)
    acol = b.sb("acol", [128, 1], F32, est)

    @blk.vector
    def _(v, w=cdve.v):
        v.wait_ge(cdve.h, w)
        v.memset(s_true[:], 0.0).then_inc(cdve.h, 1)
    cdve.v += 1

    for j in range(3):
        aev = pa.event(1)
        @blk.scalar
        def _(s, j=j, aev=aev):
            s.dma_start(out=acol[:], in_=sh_A.ap()[j]).then_inc(aev[0], 16)
        @blk.vector
        def _(v, j=j, aev=aev):
            v.wait_ge(*aev)
            v.tensor_scalar(acol[:], acol[:], use_sb[:, j:j + 1], None,
                            ALU.mult).then_inc(cdve.h, 1)
        cdve.v += 1
        @blk.scalar
        def _(s, wait=cdve.v):
            s.wait_ge(cdve.h, wait)
            s.activation(eacol[:], acol[:], AF.Exp).then_inc(cact.h, 1)
        cact.v += 1
        @blk.tensor
        def _(t, wait=cact.v):
            t.wait_ge(cact.h, wait)
            t.transpose(ps_e[0:1, 0:128], eacol[:], idf_sb[:]).then_inc(cpe.h, 1)
        cpe.v += 1
        @blk.vector
        def _(v, wait=cpe.v):
            v.wait_ge(cpe.h, wait)
            v.tensor_copy(earow[:], ps_e[0:1, 0:128]).then_inc(cdve.h, 1)
        cdve.v += 1
        @blk.tensor
        def _(t, wait=cdve.v):
            t.wait_ge(cdve.h, wait)
            t.matmul(ps_e[:, 0:128], ones_sb[:], earow[:], start=True,
                     stop=True).then_inc(cpe.h, 1)
        cpe.v += 1
        @blk.vector
        def _(v, wait=cpe.v):
            v.wait_ge(cpe.h, wait)
            v.tensor_copy(eah[:], ps_e[:, 0:128]).then_inc(cdve.h, 1)
        cdve.v += 1
        eah_ok2 = cdve.v
        for sl in range(16):
            sb_ = sl % 2
            sev = pa.event(1)
            @blk.scalar
            def _(s, j=j, sl=sl, sb_=sb_, sev=sev, w=cdve.v):
                if sl >= 2:
                    s.wait_ge(cdve.h, w)
                s.dma_start(out=shs_sl[:, sb_, :],
                            in_=sh_S.ap()[j][:, sl * 512:(sl + 1) * 512]
                            ).then_inc(sev[0], 16)
            @blk.vector
            def _(v, j=j, sl=sl, sb_=sb_, sev=sev, ew=eah_ok2, v0=cdve.v):
                v.wait_ge(*sev)
                v.wait_ge(cdve.h, max(ew, v0))
                s2 = slice(sl * 512, (sl + 1) * 512)
                v.tensor_tensor(
                    tbuf[:].rearrange("p (h q) -> p h q", h=8),
                    s_true[:, s2].rearrange("p (h q) -> p h q", h=8),
                    eah[:, sl * 8:(sl + 1) * 8].unsqueeze(2).to_broadcast([128, 8, PD]),
                    ALU.mult).then_inc(cdve.h, 1)
                v.wait_ge(cdve.h, v0 + 1)
                v.scalar_tensor_tensor(s_true[:, s2], shs_sl[:, sb_, :],
                                       use_sb[:, j:j + 1], tbuf[:],
                                       ALU.mult, ALU.add).then_inc(cdve.h, 1)
            cdve.v += 2
    est.close()

    if stop == "ecomb":
        join_all()
        sst.close()
        st.close()
        return b

    dst_ = ExitStack()
    prev_bf = b.sb("prev_bf", [128, INTER], BF16, dst_)
    y_sb = b.sb("y_sb", [128, INTER], F32, dst_)
    gate_c = b.sb("gate_c", [128, 64, CH], BF16, dst_)
    hs_c = b.sb("hs_c", [128, 64, CH], BF16, dst_)
    zrd = b.sb("zrd", [128, INTER], BF16, dst_)
    ball = b.sb("ball", [128, G, NST], BF16, dst_)
    call = b.sb("call", [128, G, NST], BF16, dst_)
    acsT = b.sb("acsT", [128, CH], F32, dst_)
    arow = b.sb("arow", [1, 4096], F32, dst_)
    gm_sb = b.sb("gm_sb", [128, 1024], BF16, dst_)
    eb_sb = b.sb("eb_sb", [128, 1024], BF16, dst_)
    t2_sb = b.sb("t2_sb", [128, 1024], F32, dst_)
    r8_bf = b.sb("r8_bf", [128, 1024], BF16, dst_)
    c2_bf = b.sb("c2_bf", [128, 1024], BF16, dst_)
    sgbuf = b.sb("sgbuf", [128, 1024], F32, dst_)
    vall = b.sb("vall", [1, 1024], F32, dst_)
    vr1 = b.sb("vr1", [1, 128], F32, dst_)
    vr2 = b.sb("vr2", [1, 128], F32, dst_)
    rsx = b.sb("rsx", [128, 1024], F32, dst_)
    onesP = b.sb("onesP", [128, 1], F32, dst_)

    @blk.vector
    def _(v):
        v.memset(onesP[:], 1.0).then_inc(cdve.h, 1)
    cdve.v += 1

    dnch = own_nch
    for c in range(dnch):
        dtev = pa.event(1)
        @blk.scalar
        def _(s, c=c, dtev=dtev, wpe=cpe.v, wdve=cdve.v):
            s.wait_ge(cpe.h, wpe)
            s.wait_ge(cdve.h, wdve)
            s.dma_start(out=dtl[:], in_=sp_dto.ap()[:, c * CH:(c + 1) * CH]
                        ).then_inc(dtev[0], 16)
        @blk.vector
        def _(v, dtev=dtev, v0=cdve.v):
            v.wait_ge(*dtev)
            v.tensor_scalar(adt[:], dtl[:], nega_sb[:, 0:1], None,
                            ALU.mult).then_inc(cdve.h, 1)
            v.wait_ge(cdve.h, v0 + 1)
            v.tensor_tensor_scan(acs[:], adt[:], zeros_sb[:, 0:128], 0.0,
                                 ALU.add, ALU.add).then_inc(cdve.h, 1)
        cdve.v += 2
        dve_acs = cdve.v
        @blk.scalar
        def _(s, wait=dve_acs):
            s.wait_ge(cdve.h, wait)
            s.activation(eacol[:], acs[:, CH - 1:CH], AF.Exp).then_inc(cact.h, 1)
        cact.v += 1
        @blk.tensor
        def _(t, wait=cact.v):
            t.wait_ge(cact.h, wait)
            t.transpose(ps_e[0:1, 0:128], eacol[:], idf_sb[:]).then_inc(cpe.h, 1)
        cpe.v += 1
        @blk.vector
        def _(v, wait=cpe.v):
            v.wait_ge(cpe.h, wait)
            v.tensor_copy(earow[:], ps_e[0:1, 0:128]).then_inc(cdve.h, 1)
        cdve.v += 1
        @blk.tensor
        def _(t, wait=cdve.v):
            t.wait_ge(cdve.h, wait)
            t.matmul(ps_e[:, 0:128], ones_sb[:], earow[:], start=True,
                     stop=True).then_inc(cpe.h, 1)
        cpe.v += 1
        @blk.vector
        def _(v, wait=cpe.v):
            v.wait_ge(cpe.h, wait)
            v.tensor_copy(eah[:], ps_e[:, 0:128]).then_inc(cdve.h, 1)
        cdve.v += 1
        eah_ok = cdve.v
        @blk.tensor
        def _(t, wait=cdve.v):
            t.wait_ge(cdve.h, wait)
            t.transpose(ps_e[:, 128:256], acs[:], idf_sb[:]).then_inc(cpe.h, 1)
        cpe.v += 1
        @blk.vector
        def _(v, wait=cpe.v):
            v.wait_ge(cpe.h, wait)
            v.tensor_copy(acsT[:], ps_e[:, 128:256]).then_inc(cdve.h, 1)
        cdve.v += 1
        a1 = pa.event(1)
        @blk.scalar
        def _(s, wait=dve_acs, a1=a1, pw=m.get("afl_free", 0)):
            s.wait_ge(cdve.h, wait)
            if pw:
                s.wait_ge(cpe.h, pw)
            s.dma_start(out=sp_aflat.ap(), in_=acs[:]).then_inc(a1[0], 16)
        @blk.vector
        def _(v, w=cdve.v):
            v.wait_ge(cdve.h, w)
            v.tensor_copy(prev_bf[:], s_true[:]).then_inc(cdve.h, 1)
        cdve.v += 1
        prev_ok = cdve.v
        bce = pa.event(2)
        gse = pa.event(2)
        ze = pa.event(1)
        @blk.scalar
        def _(s, c=c, bce=bce, gse=gse, ze=ze):
            s.dma_start(out=ball[:], in_=sp_B.ap()
                        .rearrange("(g n) t -> n g t", n=128)[:, :, c * CH:(c + 1) * CH]
                        ).then_inc(bce[0], 16)
            s.dma_start(out=call[:], in_=sp_C.ap()
                        .rearrange("(g n) t -> n g t", n=128)[:, :, c * CH:(c + 1) * CH]
                        ).then_inc(bce[0], 16)
            s.dma_start(out=gate_c[:], in_=sp_gate.ap()
                        .rearrange("(jb p) t -> p jb t", p=128)[:, :, c * CH:(c + 1) * CH]
                        ).then_inc(gse[0], 16)
            s.dma_start(out=hs_c[:], in_=sp_hs.ap()
                        .rearrange("(jb p) t -> p jb t", p=128)[:, :, c * CH:(c + 1) * CH]
                        ).then_inc(gse[0], 16)
            s.dma_start(out=zrd[:], in_=sp_Z.ap()[c]).then_inc(ze[0], 16)
        xev = psy.event(1)
        @blk.sync
        def _(s, c=c, xev=xev, gate=cdve.v):
            s.wait_ge(cdve.h, gate)
            s.dma_start_transpose(xt[:], sp_hs.ap()[:, c * CH:(c + 1) * CH]
                                  ).then_inc(xev[0], 16)
        @blk.tensor
        def _(t, wait=cdve.v, dtev=dtev):
            t.wait_ge(cdve.h, wait)
            t.transpose(ps_e[:, 256:384], dtl[:], idf_sb[:]).then_inc(cpe.h, 1)
        cpe.v += 1
        @blk.vector
        def _(v, wait=cpe.v):
            v.wait_ge(cpe.h, wait)
            v.tensor_copy(dtt_bf[:], ps_e[:, 256:384]).then_inc(cdve.h, 1)
        cdve.v += 1
        @blk.vector
        def _(v, xev=xev, v0=cdve.v):
            v.wait_ge(*xev)
            v.wait_ge(cdve.h, v0)
            v.tensor_tensor(
                xt[:].rearrange("p (h q) -> p h q", h=H),
                xt[:].rearrange("p (h q) -> p h q", h=H),
                dtt_bf[:].unsqueeze(2).to_broadcast([128, H, PD]),
                ALU.mult).then_inc(cdve.h, 1)
        cdve.v += 1
        x_ok = cdve.v
        @blk.tensor
        def _(t, bce=bce, gfree=m["bank_free"].get("gm", 0)):
            t.wait_ge(*bce)
            if gfree:
                t.wait_ge(cdve.h, gfree)
            for g2 in range(G):
                bank = ps[2] if g2 < 4 else ps[3]
                ins = t.matmul(bank[:, (g2 % 4) * 128:(g2 % 4 + 1) * 128],
                               ball[:, g2, :], call[:, g2, :], start=True, stop=True)
                if g2 in (3, 7):
                    ins.then_inc(cpe.h, 1)
        cpe.v += 2
        @blk.vector
        def _(v, wait=cpe.v, v0=cdve.v):
            v.wait_ge(cpe.h, wait)
            v.tensor_tensor(
                gm_sb[:, 0:512].rearrange("p (a l) -> p a l", a=4),
                ps[2][:].rearrange("p (a l) -> p a l", a=4),
                tri_sb[:].unsqueeze(1).to_broadcast([128, 4, 128]),
                ALU.mult).then_inc(cdve.h, 1)
            v.wait_ge(cdve.h, v0 + 1)
            v.tensor_tensor(
                gm_sb[:, 512:1024].rearrange("p (a l) -> p a l", a=4),
                ps[3][:].rearrange("p (a l) -> p a l", a=4),
                tri_sb[:].unsqueeze(1).to_broadcast([128, 4, 128]),
                ALU.mult).then_inc(cdve.h, 1)
        cdve.v += 2
        m["bank_free"]["gm"] = cdve.v
        gm_ok = cdve.v

        for bb in range(16):
            rbw = m["bank_free"].get("rb", 0)
            af = m.get(("arow", bb % 4), 0)
            rev = psy.event(1)
            @blk.sync
            def _(s, bb=bb, a1=a1, af=af, rev=rev):
                s.wait_ge(*a1)
                if af:
                    s.wait_ge(cpe.h, af)
                s.dma_start(out=arow[0:1, (bb % 4) * 1024:(bb % 4 + 1) * 1024],
                            in_=sp_aflat.ap().rearrange("(r q) l -> r (q l)", r=16)
                            [bb:bb + 1, :]).then_inc(rev[0], 16)
            @blk.tensor
            def _(t, bb=bb, rbw=rbw, rev=rev):
                t.wait_ge(*rev)
                if rbw:
                    t.wait_ge(cdve.h, rbw)
                off = (bb % 4) * 1024
                t.matmul(ps[4][:], ones_sb[:], arow[0:1, off:off + 512],
                         start=True, stop=True).then_inc(cpe.h, 1)
                t.matmul(ps[5][:], ones_sb[:], arow[0:1, off + 512:off + 1024],
                         start=True, stop=True).then_inc(cpe.h, 1)
            cpe.v += 2
            rb_pe = cpe.v
            m[("arow", bb % 4)] = cpe.v
            @blk.scalar
            def _(s, wait=rb_pe, a0=cact.v):
                s.wait_ge(cpe.h, wait)
                s.activation(eb_sb[:, 0:512], ps[4][:], AF.Exp).then_inc(cact.h, 1)
                s.activation(eb_sb[:, 512:1024], ps[5][:], AF.Exp).then_inc(cact.h, 1)
            cact.v += 2
            @blk.vector
            def _(v, bb=bb, wait=rb_pe, aw=cact.v, v0=cdve.v):
                v.wait_ge(cpe.h, wait)
                v.wait_ge(cact.h, aw)
                v.tensor_tensor(
                    t2_sb[:, 0:512].rearrange("p (a l) -> p a l", a=4),
                    ps[4][:].rearrange("p (a l) -> p a l", a=4),
                    acsT[:, bb * 8:bb * 8 + 4].unsqueeze(2).to_broadcast([128, 4, 128]),
                    ALU.subtract).then_inc(cdve.h, 1)
                v.wait_ge(cdve.h, v0 + 1)
                v.tensor_tensor(
                    t2_sb[:, 512:1024].rearrange("p (a l) -> p a l", a=4),
                    ps[5][:].rearrange("p (a l) -> p a l", a=4),
                    acsT[:, bb * 8 + 4:bb * 8 + 8].unsqueeze(2).to_broadcast([128, 4, 128]),
                    ALU.subtract).then_inc(cdve.h, 1)
                v.wait_ge(cdve.h, v0 + 2)
                v.tensor_scalar(t2_sb[:], t2_sb[:], 0.0, None,
                                ALU.min).then_inc(cdve.h, 1)
            cdve.v += 3
            m["bank_free"]["rb"] = cdve.v
            @blk.scalar
            def _(s, wait=cdve.v):
                s.wait_ge(cdve.h, wait)
                s.activation(t2_sb[:], t2_sb[:], AF.Exp).then_inc(cact.h, 1)
            cact.v += 1
            @blk.vector
            def _(v, wait=cact.v, gw=gm_ok, v0=cdve.v):
                v.wait_ge(cact.h, wait)
                v.wait_ge(cdve.h, gw)
                v.tensor_tensor(r8_bf[:], t2_sb[:], gm_sb[:], ALU.mult).then_inc(cdve.h, 1)
                v.wait_ge(cdve.h, v0 + 1)
                v.tensor_tensor(c2_bf[:], eb_sb[:], call[:].rearrange("p g n -> p (g n)"),
                                ALU.mult).then_inc(cdve.h, 1)
            cdve.v += 2
            rc_ok = cdve.v
            ybank = ps[bb % 2]
            yfree = m["bank_free"].get(("y", bb % 2), 0)
            @blk.tensor
            def _(t, bb=bb, ybank=ybank, wait=rc_ok, xw=x_ok, pw=prev_ok, yfree=yfree):
                t.wait_ge(cdve.h, max(wait, xw, pw))
                if yfree:
                    t.wait_ge(cdve.h, yfree)
                for pr in range(4):
                    h0 = 8 * bb + 2 * pr
                    col = slice(pr * 128, (pr + 1) * 128)
                    c0 = slice((2 * pr) * 128, (2 * pr + 1) * 128)
                    c1 = slice((2 * pr + 1) * 128, (2 * pr + 2) * 128)
                    t.matmul(ybank[0:64, col], xt[:, h0 * PD:(h0 + 1) * PD],
                             r8_bf[:, c0], start=True, stop=False)
                    t.matmul(ybank[0:64, col], prev_bf[:, h0 * PD:(h0 + 1) * PD],
                             c2_bf[:, c0], start=False, stop=True)
                    t.matmul(ybank[64:128, col], xt[:, (h0 + 1) * PD:(h0 + 2) * PD],
                             r8_bf[:, c1], start=True, stop=False)
                    ins = t.matmul(ybank[64:128, col], prev_bf[:, (h0 + 1) * PD:(h0 + 2) * PD],
                                   c2_bf[:, c1], start=False, stop=True)
                    if pr == 3:
                        ins.then_inc(cpe.h, 1)
            cpe.v += 1
            @blk.vector
            def _(v, bb=bb, ybank=ybank, wait=cpe.v, gse=gse, v0=cdve.v):
                v.wait_ge(cpe.h, wait)
                if bb == 0:
                    v.wait_ge(*gse)
                for pr in range(4):
                    jb = 4 * bb + pr
                    v.scalar_tensor_tensor(y_sb[:, jb * 128:(jb + 1) * 128],
                                           hs_c[:, jb, :], dcols_sb[:, jb:jb + 1],
                                           ybank[:, pr * 128:(pr + 1) * 128],
                                           ALU.mult, ALU.add).then_inc(cdve.h, 1)
            cdve.v += 4
            m["bank_free"][("y", bb % 2)] = cdve.v

        m["afl_free"] = cpe.v
        for sl in range(16):
            @blk.vector
            def _(v, sl=sl, ze=ze, ew=eah_ok, v0=cdve.v):
                if sl == 0:
                    v.wait_ge(*ze)
                v.wait_ge(cdve.h, max(ew, v0))
                s2 = slice(sl * 512, (sl + 1) * 512)
                v.tensor_tensor(
                    tbuf[:].rearrange("p (h q) -> p h q", h=8),
                    s_true[:, s2].rearrange("p (h q) -> p h q", h=8),
                    eah[:, sl * 8:(sl + 1) * 8].unsqueeze(2).to_broadcast([128, 8, PD]),
                    ALU.mult).then_inc(cdve.h, 1)
                v.wait_ge(cdve.h, v0 + 1)
                v.tensor_tensor(s_true[:, s2], tbuf[:], zrd[:, s2],
                                ALU.add).then_inc(cdve.h, 1)
            cdve.v += 2

        for q in range(8):
            @blk.scalar
            def _(s, q=q, w=cdve.v, a0=cact.v):
                s.wait_ge(cdve.h, w)
                s.activation(sgbuf[:], gate_c[:].rearrange("p jb t -> p (jb t)")
                             [:, q * 1024:(q + 1) * 1024], AF.Silu).then_inc(cact.h, 1)
            cact.v += 1
            @blk.vector
            def _(v, q=q, wait=cact.v, v0=cdve.v):
                v.wait_ge(cact.h, wait)
                v.wait_ge(cdve.h, v0)
                v.tensor_tensor(y_sb[:, q * 1024:(q + 1) * 1024],
                                y_sb[:, q * 1024:(q + 1) * 1024], sgbuf[:],
                                ALU.mult).then_inc(cdve.h, 1)
            cdve.v += 1

        for g2 in range(8):
            @blk.scalar
            def _(s, g2=g2, w=cdve.v):
                s.wait_ge(cdve.h, w)
                s.activation(sgbuf[:], y_sb[:, g2 * 1024:(g2 + 1) * 1024],
                             AF.Square).then_inc(cact.h, 1)
            cact.v += 1
            pfree = m["bank_free"].get("rms", 0)
            @blk.tensor
            def _(t, wait=cact.v, pfree=pfree):
                t.wait_ge(cact.h, wait)
                if pfree:
                    t.wait_ge(cdve.h, pfree)
                t.matmul(ps[4][0:1, :], onesP[:], sgbuf[:, 0:512], start=True,
                         stop=True).then_inc(cpe.h, 1)
                t.matmul(ps[5][0:1, :], onesP[:], sgbuf[:, 512:1024], start=True,
                         stop=True).then_inc(cpe.h, 1)
            cpe.v += 2
            @blk.vector
            def _(v, g2=g2, wait=cpe.v, v0=cdve.v):
                v.wait_ge(cpe.h, wait)
                v.tensor_reduce(vr1[:], ps[4][0:1, :].rearrange("a (jb l) -> a l jb", jb=4),
                                mybir.AxisListType.X, ALU.add).then_inc(cdve.h, 1)
                v.wait_ge(cdve.h, v0 + 1)
                v.tensor_reduce(vr2[:], ps[5][0:1, :].rearrange("a (jb l) -> a l jb", jb=4),
                                mybir.AxisListType.X, ALU.add).then_inc(cdve.h, 1)
                v.wait_ge(cdve.h, v0 + 2)
                v.tensor_tensor(vall[0:1, g2 * 128:(g2 + 1) * 128], vr1[:], vr2[:],
                                ALU.add).then_inc(cdve.h, 1)
            cdve.v += 3
            m["bank_free"]["rms"] = cdve.v
        @blk.scalar
        def _(s, w=cdve.v, a0=cact.v):
            s.wait_ge(cdve.h, w)
            s.activation(vall[:], vall[:], AF.Ln, scale=1.0 / 1024.0,
                         bias=eps_sb[0:1, :]).then_inc(cact.h, 1)
            s.wait_ge(cact.h, a0 + 1)
            s.activation(vall[:], vall[:], AF.Exp, scale=-0.5).then_inc(cact.h, 1)
        cact.v += 2
        @blk.tensor
        def _(t, wait=cact.v, pfree=m["bank_free"].get("rms", 0)):
            t.wait_ge(cact.h, wait)
            t.wait_ge(cdve.h, pfree)
            t.matmul(ps[4][:], ones_sb[:], vall[0:1, 0:512], start=True,
                     stop=True).then_inc(cpe.h, 1)
            t.matmul(ps[5][:], ones_sb[:], vall[0:1, 512:1024], start=True,
                     stop=True).then_inc(cpe.h, 1)
        cpe.v += 2
        @blk.vector
        def _(v, wait=cpe.v, v0=cdve.v):
            v.wait_ge(cpe.h, wait)
            v.tensor_copy(rsx[:, 0:512], ps[4][:]).then_inc(cdve.h, 1)
            v.wait_ge(cdve.h, v0 + 1)
            v.tensor_copy(rsx[:, 512:1024], ps[5][:]).then_inc(cdve.h, 1)
            v.wait_ge(cdve.h, v0 + 2)
            v.tensor_tensor(
                y_sb[:].rearrange("p (g d l) -> p g d l", g=8, d=8),
                y_sb[:].rearrange("p (g d l) -> p g d l", g=8, d=8),
                rsx[:].rearrange("p (g l) -> p g l", g=8).unsqueeze(2)
                .to_broadcast([128, 8, 8, 128]),
                ALU.mult).then_inc(cdve.h, 1)
            v.wait_ge(cdve.h, v0 + 3)
            v.tensor_tensor(
                y_sb[:].rearrange("p (jb t) -> p jb t", jb=64),
                y_sb[:].rearrange("p (jb t) -> p jb t", jb=64),
                nw_sb[:].unsqueeze(2).to_broadcast([128, 64, CH]),
                ALU.mult).then_inc(cdve.h, 1)
        cdve.v += 4
        m["bank_free"]["rms"] = cdve.v
        nev = pg.event(1)
        @blk.gpsimd
        def _(g, c=c, w=cdve.v, nev=nev):
            g.wait_ge(cdve.h, w)
            g.dma_start(out=sp_norm.ap().rearrange("(jb p) t -> p jb t", p=128)
                        [:, :, c * CH:(c + 1) * CH],
                        in_=y_sb[:].rearrange("p (jb t) -> p jb t", jb=64)
                        ).then_inc(nev[0], 16)
        m["nrm_spill"] = nev
        @blk.vector
        def _(v, nev=nev):
            v.wait_ge(*nev)

    dst_.close()

    if stop == "dscan":
        join_all()
        sst.close()
        st.close()
        return b

    join_all()
    sst.close()
    pwv = emit_poll([0, 1])
    for eng in ("vector", "scalar", "tensor", "gpsimd", "sync"):
        @getattr(blk, eng)
        def _(e, w=pwv):
            e.wait_ge(chs.h, w)

    cst = ExitStack()
    xnt_c = b.sb("xnt_c", [128, 32, WTOK], BF16, cst)
    wg_sb = b.sb("wg_sb", [128, 3, 32, 128], BF16, cst)
    wu_sb = b.sb("wu_sb", [128, 3, 32, 128], BF16, cst)
    sg_c = b.sb("sg_c", [128, 1024], F32, cst)
    h_bf = b.sb("h_bf", [128, 2, 1024], BF16, cst)

    n_th = 2 if dev_own is None else dev_own
    c0_dve = cdve.v
    for th in range(n_th):
        xev = psy.event(32)
        @blk.sync
        def _(s, th=th, xev=xev, pw=m.get("xnt_free", 0)):
            if pw:
                s.wait_ge(cpe.h, pw)
            for kc in range(32):
                s.dma_start_transpose(
                    xnt_c[:, kc, :],
                    xn_scr.ap()[th * WTOK:(th + 1) * WTOK, kc * 128:(kc + 1) * 128]
                    ).then_inc(xev[0], 16)
        for fb in range(86):
            p2 = fb % 2
            p3 = fb % 3
            wgev = pa.event(1)
            wuev = psy.event(1)
            @blk.scalar
            def _(s, fb=fb, p3=p3, wgev=wgev,
                  fw=m.get(("cwf", p3), 0)):
                if fw:
                    s.wait_ge(cpe.h, fw)
                s.dma_start(out=wg_sb[:, p3],
                            in_=sh_wg.ap()[:, fb]).then_inc(wgev[0], 16)
            @blk.sync
            def _(s, fb=fb, p3=p3, wuev=wuev,
                  fw=m.get(("cwf", p3), 0)):
                if fw:
                    s.wait_ge(cpe.h, fw)
                s.dma_start(out=wu_sb[:, p3],
                            in_=sh_wu.ap()[:, fb]).then_inc(wuev[0], 16)
            @blk.tensor
            def _(t, fb=fb, p3=p3, xev=xev, wgev=wgev, wuev=wuev,
                  aw=m.get("c_act", 0), pw=m.get("c_pool", 0), d0=c0_dve):
                t.wait_ge(*xev)
                t.wait_ge(*wgev)
                t.wait_ge(*wuev)
                if aw:
                    t.wait_ge(cact.h, aw)
                if pw:
                    t.wait_ge(cdve.h, pw)
                else:
                    t.wait_ge(cdve.h, d0)
                for tq in range(2):
                    for kc in range(32):
                        ins = t.matmul(ps[tq][:], wg_sb[:, p3, kc, :],
                                       xnt_c[:, kc, tq * 512:(tq + 1) * 512],
                                       start=(kc == 0), stop=(kc == 31))
                        if kc == 31:
                            ins.then_inc(cpe.h, 1)
                    for kc in range(32):
                        ins = t.matmul(ps[2 + tq][:], wu_sb[:, p3, kc, :],
                                       xnt_c[:, kc, tq * 512:(tq + 1) * 512],
                                       start=(kc == 0), stop=(kc == 31))
                        if kc == 31:
                            ins.then_inc(cpe.h, 1)
            cpe.v += 4
            m[("cwf", p3)] = cpe.v
            m["xnt_free"] = cpe.v
            @blk.scalar
            def _(s, wait=cpe.v, a0=cact.v):
                s.wait_ge(cpe.h, wait)
                s.activation(sg_c[:, 0:512], ps[0][:], AF.Silu).then_inc(cact.h, 1)
                s.activation(sg_c[:, 512:1024], ps[1][:], AF.Silu).then_inc(cact.h, 1)
            cact.v += 2
            m["c_act"] = cact.v
            hev = pg.event(1)
            @blk.vector
            def _(v, p2=p2, aw=cact.v, v0=cdve.v,
                  ow=m.get(("chf", p2), None)):
                v.wait_ge(cact.h, aw)
                if ow is not None:
                    v.wait_ge(*ow)
                v.tensor_tensor(h_bf[:, p2, 0:512], sg_c[:, 0:512], ps[2][:],
                                ALU.mult).then_inc(cdve.h, 1)
                v.wait_ge(cdve.h, v0 + 1)
                v.tensor_tensor(h_bf[:, p2, 512:1024], sg_c[:, 512:1024], ps[3][:],
                                ALU.mult).then_inc(cdve.h, 1)
            cdve.v += 2
            m["c_pool"] = cdve.v
            @blk.gpsimd
            def _(g, th=th, fb=fb, p2=p2, w=cdve.v, hev=hev):
                g.wait_ge(cdve.h, w)
                g.dma_start(out=sp_h.ap()[fb * 128:(fb + 1) * 128,
                                          th * WTOK:(th + 1) * WTOK],
                            in_=h_bf[:, p2]).then_inc(hev[0], 16)
            m[("chf", p2)] = hev
    cst.close()

    c2st = ExitStack()
    hblk_sb = b.sb("hblk_sb", [128, 86, 512], BF16, c2st)
    wd_sb = b.sb("wd_sb", [128, 4, 512], BF16, c2st)
    sdn_sb = b.sb("sdn_sb", [128, 2, 512], F32, c2st)

    n_tq = 4 if dev_own is None else 2 * dev_own
    c_end = (cpe.v, cdve.v, m[("chf", 0)], m.get(("chf", 1)))
    for tq in range(n_tq):
        hbev = pa.event(1)
        @blk.scalar
        def _(s, tq=tq, hbev=hbev, pw=m.get("hblk_free", 0), c_end=c_end):
            if pw:
                s.wait_ge(cpe.h, pw)
            else:
                s.wait_ge(cpe.h, c_end[0])
                s.wait_ge(cdve.h, c_end[1])
                s.wait_ge(*c_end[2])
                if c_end[3] is not None:
                    s.wait_ge(*c_end[3])
            s.dma_start(out=hblk_sb[:], in_=sp_h.ap()
                        .rearrange("(kb p) t -> p kb t", p=128)
                        [:, :, tq * 512:(tq + 1) * 512]).then_inc(hbev[0], 16)
        for db in range(8):
            for kb in range(86):
                p2 = kb % 4
                wdev = (pa if kb % 2 == 0 else pg).event(1)
                eng_ld = blk.scalar if kb % 2 == 0 else blk.gpsimd
                @eng_ld
                def _(s, db=db, kb=kb, p2=p2, wdev=wdev,
                      fw=m.get(("c2wf", p2), 0)):
                    if fw:
                        s.wait_ge(cpe.h, fw)
                    s.dma_start(out=wd_sb[:, p2], in_=sh_wd.ap()
                                [kb * 128:(kb + 1) * 128, db * 512:(db + 1) * 512]
                                ).then_inc(wdev[0], 16)
                @blk.tensor
                def _(t, db=db, kb=kb, p2=p2, wdev=wdev, hbev=hbev,
                      pw=m.get("c2_pool", 0)):
                    t.wait_ge(*wdev)
                    if kb == 0:
                        t.wait_ge(*hbev)
                        if pw:
                            t.wait_ge(cdve.h, pw)
                    for t4 in range(4):
                        ins = t.matmul(ps[t4][:], hblk_sb[:, kb, t4 * 128:(t4 + 1) * 128],
                                       wd_sb[:, p2], start=(kb == 0), stop=(kb == 85))
                        if t4 == 3:
                            ins.then_inc(cpe.h, 1)
                cpe.v += 1
                m[("c2wf", p2)] = cpe.v
            m["hblk_free"] = cpe.v
            for t4 in range(4):
                p2 = t4 % 2
                dnev = pg.event(1)
                @blk.vector
                def _(v, t4=t4, p2=p2, w=cpe.v,
                      ow=m.get(("dnf", p2), None)):
                    v.wait_ge(cpe.h, w)
                    if ow is not None:
                        v.wait_ge(*ow)
                    v.tensor_copy(sdn_sb[:, p2], ps[t4][:]).then_inc(cdve.h, 1)
                cdve.v += 1
                @blk.gpsimd
                def _(g, tq=tq, db=db, t4=t4, p2=p2, w=cdve.v, dnev=dnev):
                    g.wait_ge(cdve.h, w)
                    g.dma_start(out=sp_down.ap()
                                [tq * 512 + t4 * 128:tq * 512 + (t4 + 1) * 128,
                                 db * 512:(db + 1) * 512],
                                in_=sdn_sb[:, p2]).then_inc(dnev[0], 16)
                m[("dnf", p2)] = dnev
            m["c2_pool"] = cdve.v
    c2st.close()

    if stop == "ffn":
        join_all()
        st.close()
        return b

    fst = ExitStack()
    nt_res = b.sb("nt_res", [128, 64, 768], BF16, fst)
    wo_sb = b.sb("wo_sb", [128, 4, 512], BF16, fst)
    xres = b.sb("xres", [128, 6, 512], F32, fst)
    dres = b.sb("dres", [128, 6, 512], F32, fst)
    ostg = b.sb("ostg", [128, 2, 512], F32, fst)

    n_t128 = (T_OWN if dev_own is None else dev_own * WTOK) // 128
    tgs = []
    t0_ = 0
    while t0_ < n_t128:
        g_ = min(6, n_t128 - t0_)
        tgs.append((t0_, g_))
        t0_ += g_
    c2_end = (cpe.v, cdve.v, m[("dnf", 0)], m[("dnf", 1)])
    for gi, (tb0, ntg) in enumerate(tgs):
        ntev = pa.event(1)
        @blk.scalar
        def _(s, tb0=tb0, ntg=ntg, ntev=ntev, gi=gi,
              pw=m.get("nt_free", 0), c2_end=c2_end):
            if gi == 0:
                s.wait_ge(cpe.h, c2_end[0])
                s.wait_ge(cdve.h, c2_end[1])
                s.wait_ge(*c2_end[2])
                s.wait_ge(*c2_end[3])
            if pw:
                s.wait_ge(cpe.h, pw)
            s.dma_start(out=nt_res[:, :, 0:ntg * 128], in_=sp_norm.ap()
                        .rearrange("(jb p) t -> p jb t", p=128)
                        [:, :, tb0 * 128:(tb0 + ntg) * 128]).then_inc(ntev[0], 16)
        for db in range(8):
            xdev = pa.event(2)
            @blk.scalar
            def _(s, tb0=tb0, ntg=ntg, db=db, xdev=xdev,
                  vw=m.get("f_dve", 0)):
                if vw:
                    s.wait_ge(cdve.h, vw)
                s.dma_start(out=xres[:, 0:ntg, :], in_=own_x
                            [3 + tb0 * 128:3 + (tb0 + ntg) * 128,
                             db * 512:(db + 1) * 512]
                            .rearrange("(n p) d -> p n d", p=128)
                            ).then_inc(xdev[0], 16)
                s.dma_start(out=dres[:, 0:ntg, :], in_=sp_down.ap()
                            [tb0 * 128:(tb0 + ntg) * 128, db * 512:(db + 1) * 512]
                            .rearrange("(n p) d -> p n d", p=128)
                            ).then_inc(xdev[0], 16)
            for jb in range(64):
                p2 = jb % 4
                woev = pg.event(1)
                @blk.gpsimd
                def _(g, jb=jb, db=db, p2=p2, woev=woev,
                      fw=m.get(("fwf", p2), 0)):
                    if fw:
                        g.wait_ge(cpe.h, fw)
                    g.dma_start(out=wo_sb[:, p2], in_=w_outT
                                [jb * 128:(jb + 1) * 128, db * 512:(db + 1) * 512]
                                ).then_inc(woev[0], 16)
                @blk.tensor
                def _(t, jb=jb, ntg=ntg, p2=p2, woev=woev, ntev=ntev,
                      vw=m.get("f_dve", 0)):
                    t.wait_ge(*woev)
                    if jb == 0:
                        t.wait_ge(*ntev)
                        if vw:
                            t.wait_ge(cdve.h, vw)
                    for tt in range(ntg):
                        ins = t.matmul(ps[tt][:], nt_res[:, jb, tt * 128:(tt + 1) * 128],
                                       wo_sb[:, p2], start=(jb == 0), stop=(jb == 63))
                        if tt == ntg - 1:
                            ins.then_inc(cpe.h, 1)
                cpe.v += 1
                m[("fwf", p2)] = cpe.v
            m["nt_free"] = cpe.v
            for tt in range(ntg):
                p2 = tt % 2
                oev = pa.event(1)
                @blk.vector
                def _(v, tb0=tb0, tt=tt, db=db, p2=p2, w=cpe.v, xdev=xdev,
                      v0=cdve.v, ow=m.get(("onf", p2), None)):
                    v.wait_ge(cpe.h, w)
                    v.wait_ge(*xdev)
                    if ow is not None:
                        v.wait_ge(*ow)
                    v.tensor_tensor(ostg[:, p2], ps[tt][:], xres[:, tt, :],
                                    ALU.add).then_inc(cdve.h, 1)
                    v.wait_ge(cdve.h, v0 + 1)
                    v.tensor_tensor(ostg[:, p2], ostg[:, p2], dres[:, tt, :],
                                    ALU.add).then_inc(cdve.h, 1)
                cdve.v += 2
                @blk.scalar
                def _(s, tb0=tb0, tt=tt, db=db, p2=p2, w=cdve.v, oev=oev):
                    s.wait_ge(cdve.h, w)
                    s.dma_start(out=out[(tb0 + tt) * 128:(tb0 + tt + 1) * 128,
                                        db * 512:(db + 1) * 512],
                                in_=ostg[:, p2]).then_inc(oev[0], 16)
                m[("onf", p2)] = oev
            m["f_dve"] = cdve.v

    fst.close()
    join_all()
    st.close()
    return b


def _prep_host(inputs):
    x = np.ascontiguousarray(np.asarray(inputs["L0_attn_hidden_states"]).reshape(L, HID))
    f32 = np.float32
    w_in = np.asarray(inputs["L0_attn_in_proj_w"]).astype(f32)
    w_int = np.ascontiguousarray(
        w_in.reshape(PROJ // 128, 128, HID // 128, 128).transpose(0, 3, 2, 1))
    w_outT = np.ascontiguousarray(np.asarray(inputs["L0_attn_out_proj_w"]).T.astype(f32))
    conv_w = np.asarray(inputs["L0_attn_conv_w"]).reshape(CONV_DIM, KTAP)
    conv_w4 = np.ascontiguousarray(conv_w.reshape(80, 128, KTAP).transpose(1, 0, 2).astype(f32))
    conv_bc = np.ascontiguousarray(
        np.asarray(inputs["L0_attn_conv_b"]).reshape(80, 128).T.astype(f32))
    dtb_col = np.asarray(inputs["L0_attn_dt_bias"]).reshape(128, 1).astype(f32)
    alog_col = np.asarray(inputs["L0_attn_A_log"]).reshape(128, 1).astype(f32)
    dv = np.asarray(inputs["L0_attn_D"]).astype(f32)
    d_cols = np.ascontiguousarray(
        np.repeat(dv.reshape(64, 2), 64, axis=1).reshape(64, 128).T)
    nw = np.asarray(inputs["L0_attn_norm_weight"]).astype(f32)
    normw_cols = np.ascontiguousarray(nw.reshape(64, 128).T)
    trimask = np.ascontiguousarray(np.triu(np.ones((128, 128), f32)))
    ones_rw = np.ones((1, 128), f32)
    ident = np.eye(128).astype(ml_dtypes.bfloat16)
    ident_f = np.eye(128, dtype=f32)

    def lowb(a):
        a = np.ascontiguousarray(np.asarray(a).astype(np.int32))
        return a.view(np.uint8).reshape(a.shape[0], a.shape[1], 4)[:, :, 0]

    def zsT(z, s):
        return np.vstack([np.asarray(z).T.astype(f32), np.asarray(s).T.astype(f32)])

    gq_b = np.asarray(lowb(inputs["L0_ffn_gate_Wq"]).T)
    uq_b = np.asarray(lowb(inputs["L0_ffn_up_Wq"]).T)
    dq_b = np.asarray(lowb(inputs["L0_ffn_down_Wq"]).T)
    g_zsT_f = zsT(inputs["L0_ffn_gate_zero"], inputs["L0_ffn_gate_scale"])
    u_zsT_f = zsT(inputs["L0_ffn_up_zero"], inputs["L0_ffn_up_scale"])
    d_zsT_f = zsT(inputs["L0_ffn_down_zero"], inputs["L0_ffn_down_scale"])
    ind4v = np.zeros((4, 128), f32)
    for gg in range(4):
        ind4v[gg, gg * 32:(gg + 1) * 32] = 1.0

    rng = np.random.default_rng()
    nonce = rng.integers(1, 2 ** 30, (1, 2)).astype(np.int32)

    in_maps = []
    for r in range(NCORES):
        p, s = r // 2, r % 2
        start = T_OWN * r
        halo = x[start - 3:start] if start >= 3 else np.vstack(
            [np.zeros((3 - start, HID), f32), x[0:start]])
        ox = np.ascontiguousarray(np.vstack([halo, x[start:start + T_OWN]]))
        ln = 2048 * p
        rs = s * 2048 * p
        pad = T_REC - ln
        if rs >= pad:
            padreg = x[rs - pad:rs]
        else:
            padreg = np.vstack([np.zeros((pad - rs, HID), f32), x[0:rs]])
        rx = np.ascontiguousarray(np.vstack([padreg, x[rs:rs + ln]]))
        rmask = np.zeros((NCH_REC,), f32)
        rmask[pad // CH:] = 1.0
        rmask_cols = np.ascontiguousarray(np.broadcast_to(rmask, (128, NCH_REC)))
        umask = np.array([1.0, 1.0, 0.0 if s == 0 else 1.0], f32)
        use_colsv = np.ascontiguousarray(np.broadcast_to(umask, (128, 3)))
        hrow = FFN_I // 2
        hrow2 = HID // 2
        in_maps.append({
            "own_x": ox, "rec_x": rx, "rmask_cols": rmask_cols, "use_cols": use_colsv,
            "nonce": nonce, "w_int": w_int, "w_outT": w_outT,
            "conv_w4": conv_w4, "conv_bc": conv_bc, "dtb_col": dtb_col,
            "alog_col": alog_col, "d_cols": d_cols, "normw_cols": normw_cols,
            "trimask": trimask, "ones_row": ones_rw, "ident_bf": ident, "ident_f": ident_f,
            "gq_pk": np.ascontiguousarray(gq_b[:, s * hrow:(s + 1) * hrow]),
            "uq_pk": np.ascontiguousarray(uq_b[:, s * hrow:(s + 1) * hrow]),
            "dq_pk": np.ascontiguousarray(dq_b[:, s * hrow2:(s + 1) * hrow2]),
            "g_zsT": np.ascontiguousarray(g_zsT_f[:, s * hrow:(s + 1) * hrow]),
            "u_zsT": np.ascontiguousarray(u_zsT_f[:, s * hrow:(s + 1) * hrow]),
            "d_zsT": np.ascontiguousarray(d_zsT_f[:, s * hrow2:(s + 1) * hrow2]),
            "ind4_i": ind4v,
        })
    return in_maps


def kernel(**inputs):
    in_maps = _prep_host(inputs)
    bobj = build()
    res = run_bass_kernel_spmd(bobj.nc, in_maps, core_ids=list(range(NCORES)))
    outs = [res.results[r]["out"] for r in range(NCORES)]
    return np.concatenate(outs, axis=0).reshape(1, L, HID).astype(np.float32)
